# version 06
# baseline (speedup 1.0000x reference)
import numpy as np
import ml_dtypes
import concourse.bass as bass
import concourse.mybir as mybir
from concourse.bass_utils import run_bass_kernel_spmd

F32 = mybir.dt.float32
BF16 = mybir.dt.bfloat16
FP16 = mybir.dt.float16
AF = mybir.ActivationFunctionType
ALU = mybir.AluOpType
AX = mybir.AxisListType

N = 4096
D = 1024
NT = N // 128
NCH = D // 128
QB = 512
NQB = N // QB
FB = 256
NFB = N // FB
DH, DL = 4096, 512
NUM_HEAVY = 1024
COOR_ITERS = 50
BISECT_ITERS = 40
EPS_LN = 1e-5

ENGINES = ["tensor", "vector", "scalar", "gpsimd", "sync"]
SELF_SAFE = {"tensor"}
RING_DEPTH = {"sync": 10, "gpsimd": 6}


class Pipe:
    def __init__(self):
        self.steps = []
        self.sem_names = (["c_tensor", "c_vector", "c_scalar", "c_gpsimd"]
                          + [f"q_sync{i}" for i in range(RING_DEPTH["sync"])]
                          + [f"q_gpsimd{i}" for i in range(RING_DEPTH["gpsimd"])])
        self.count = {s: 0 for s in self.sem_names}
        self.qpos = {"sync": 0, "gpsimd": 0}
        self.know = {e: {s: 0 for s in self.sem_names} for e in ENGINES}
        self.clk = {s: [] for s in self.sem_names}
        self.last_write = {}
        self.readers = {}
        self.nwaits = 0

    def _deps(self, reads, writes):
        m = {}
        for k in reads:
            w = self.last_write.get(k)
            if w and w[1] > m.get(w[0], (0, None))[0]:
                m[w[0]] = (w[1], w[2])
        for k in writes:
            w = self.last_write.get(k)
            if w and w[1] > m.get(w[0], (0, None))[0]:
                m[w[0]] = (w[1], w[2])
            for r in self.readers.get(k, ()):
                if r[1] > m.get(r[0], (0, None))[0]:
                    m[r[0]] = (r[1], r[2])
        return m

    def _emit(self, engine, fn, reads, writes, sem, amt, extra_dep=None):
        know = self.know[engine]
        waits = []
        m = self._deps(reads + ["BAR"], writes)
        if extra_dep:
            s, v = extra_dep
            if v > 0 and v > m.get(s, (0, None))[0]:
                m[s] = (v, None)
        own = "c_" + engine
        for s in sorted(m):
            v, c = m[s]
            if s == own and engine in SELF_SAFE:
                continue
            if know[s] >= v:
                continue
            waits.append((s, v))
            self.nwaits += 1
            know[s] = v
            if c is not None:
                for s2, v2 in c.items():
                    if v2 > know[s2]:
                        know[s2] = v2
        self.count[sem] += amt
        val = self.count[sem]
        cl = dict(know)
        cl[sem] = val
        self.clk[sem].append(cl)
        self.steps.append((engine, fn, waits, (sem, amt)))
        rec = (sem, val, cl)
        for k in writes:
            self.last_write[k] = rec
            self.readers[k] = []
        for k in reads:
            self.readers.setdefault(k, []).append(rec)

    def step(self, engine, fn, rd=(), wr=()):
        self._emit(engine, fn, list(rd), list(wr), "c_" + engine, 1)

    def dma(self, queue, out, in_, rd=(), wr=(), **kw):
        i = self.qpos[queue]
        self.qpos[queue] += 1
        depth = RING_DEPTH[queue]
        sem = f"q_{queue}{i % depth}"
        extra = (sem, 16 * (i // depth)) if i >= depth else None
        fn = (lambda o, s, kw2: (lambda e: e.dma_start(o, s, **kw2)))(out, in_, kw)
        self._emit(queue, fn, list(rd), list(wr), sem, 16, extra_dep=extra)

    def barrier(self):
        keys = (set(self.last_write) | set(self.readers)) - {"BAR"}
        self._emit("gpsimd", lambda e: None, [], list(keys) + ["BAR"], "c_gpsimd", 1)

    def fence(self, keys):
        self._emit("gpsimd", lambda e: None, [], list(keys) + ["BAR"], "c_gpsimd", 1)

    def emit(self, nc, blk, sems):
        finals = dict(self.count)
        for name in ENGINES:
            def make(nm):
                def emit_eng(eng):
                    for en, fn, waits, (sem, amt) in self.steps:
                        if en != nm:
                            continue
                        for s, v in waits:
                            eng.wait_ge(sems[s], v)
                        inst = fn(eng)
                        if inst is None:
                            inst = eng.nop()
                        inst.then_inc(sems[sem], amt)
                    if nm == "sync":
                        for s, v in finals.items():
                            if v > 0:
                                eng.wait_ge(sems[s], v)
                return emit_eng
            getattr(blk, name)(make(name))


def host_constants():
    c = {}
    bf = ml_dtypes.bfloat16
    c["IDENT"] = np.eye(128, dtype=bf)
    c["IDENT32"] = np.eye(128, dtype=np.float32)
    c["ONESROW"] = np.ones((1, 128), dtype=bf)
    c["ONESCOL"] = np.ones((128, 1), dtype=np.float32)
    c["LTRI"] = np.triu(np.ones((128, 128), dtype=np.float32))
    fraction = (2 * np.arange(0, 32) / 64).astype(np.float32)
    timescale = (np.float32(10000.0) ** fraction).astype(np.float32)
    pos = np.arange(N).astype(np.float32)
    sinu = (pos[None, :] / timescale[:, None]).astype(np.float32)
    sin = np.sin(sinu).astype(np.float32)
    cos = np.cos(sinu).astype(np.float32)
    c["COS128"] = np.concatenate([cos, cos, cos, cos], 0).astype(bf)
    c["SINSIG"] = np.concatenate([-sin, sin, -sin, sin], 0).astype(bf)
    r0 = np.zeros((64, 1024), dtype=bf)
    for m in range(64):
        r0[m, m * 16:(m + 1) * 16] = 1
    c["R0"] = r0
    c0 = np.zeros((128, 8 * 64), dtype=bf)
    for k in range(8):
        for mm in range(8):
            m = 8 * k + mm
            c0[mm * 16:(mm + 1) * 16, k * 64 + m] = 1
    c["C0"] = c0
    return c


AR_K = 0
AR_V = 32768
AR_WV = 66048
AR_QSL = 74240
AR_XSL = 78336
AR_CTX = 82432
AR_PB = 86528
AR_RT0 = 87552
AR_RT1 = 88064
AR_CTXT = 88576
AR_RSL = 89088
AR_END = 90112
AR_W1H = 0
AR_W2H = 32768
AR_HB = 65536
AR_LHB = 73728
AR_XRB = 74752
AR_LW1 = 76800
AR_LW2 = 80896
AR_WIN = 0
AR_SBF = 512
AR_HF = 1024
AR_GBF = 5120
AR_WOUT = 5632


def build_program(upto="all"):
    nc = bass.Bass("TRN2", target_bir_lowering=False)
    P = Pipe()

    def din(name, shape, dtype=F32):
        return nc.dram_tensor(name, list(shape), dtype, kind="ExternalInput").ap()

    x_in = din("x", (N, D))
    WSHAPES = [
        ("L0_attn_wq_w", (D, D)), ("L0_attn_wq_b", (D,)),
        ("L0_attn_wk_w", (D, D)), ("L0_attn_wk_b", (D,)),
        ("L0_attn_wv_w", (D, D)), ("L0_attn_wv_b", (D,)),
        ("L0_attn_wo_w", (D, D)), ("L0_attn_wo_b", (D,)),
        ("L0_ffn_routing_token", (D,)),
        ("L0_ffn_light_gamma", (D,)), ("L0_ffn_light_w1", (D, DL)),
        ("L0_ffn_light_b1", (DL,)), ("L0_ffn_light_w2", (DL, D)),
        ("L0_ffn_light_b2", (D,)),
        ("L0_ffn_heavy_gamma", (D,)), ("L0_ffn_heavy_w1", (D, DH)),
        ("L0_ffn_heavy_b1", (DH,)), ("L0_ffn_heavy_w2", (DH, D)),
        ("L0_ffn_heavy_b2", (D,)),
        ("L1_attn_Wq", (D, D)), ("L1_attn_bq", (D,)),
        ("L1_attn_Wk", (D, D)), ("L1_attn_bk", (D,)),
        ("L1_attn_Wv", (D, D)), ("L1_attn_bv", (D,)),
        ("L1_attn_Wo", (D, D)),
        ("L1_ffn_Win", (D, 64)), ("L1_ffn_bin_", (64,)),
        ("L1_ffn_W1", (64, 16)), ("L1_ffn_b1", (64, 16)),
        ("L1_ffn_W2", (64, 16)), ("L1_ffn_b2", (64,)),
        ("L1_ffn_Wout", (64, D)), ("L1_ffn_bout", (D,)),
    ]
    W = {nm: din(nm, shp) for nm, shp in WSHAPES}
    CSPEC = host_constants()
    CONST_IN = {k: din("c_" + k, v.shape,
                       BF16 if v.dtype == ml_dtypes.bfloat16 else F32)
                for k, v in CSPEC.items()}

    y_out = nc.dram_tensor("y", [N, D], F32, kind="ExternalOutput").ap()
    x_buf = nc.dram_tensor("x_buf", [N, D], F32, kind="Internal").ap()
    xprime = nc.dram_tensor("xprime", [N, D], F32, kind="Internal").ap()
    xnT_hbm = nc.dram_tensor("xnT_hbm", [NCH, 128, N], BF16, kind="Internal").ap()
    xrT_hbm = nc.dram_tensor("xrT_hbm", [NCH, 128, N], BF16, kind="Internal").ap()
    wT_hbm = {nm: nc.dram_tensor(nm + "_T", [NCH, 128, D], BF16, kind="Internal").ap()
              for nm in ["L1q", "L1k", "L1v", "L1o"]}

    sb = nc.alloc_sbuf_tensor
    ARENA = sb("ARENA", [128, AR_END], BF16)

    def AR(off, nels):
        return ARENA[:, off:off + nels]

    WST = sb("WST", [128, 2048], BF16)
    STGX = sb("STGX", [128, 3072], F32)
    STG = STGX[:, 0:2048]
    XST = STGX[:, 2048:3072]
    REPB = sb("REPB", [128, 1024], F32)
    SMALL = sb("SMALLF", [128, 64], F32)
    SBUF_BIAS = sb("SBUF_BIAS", [128, 128], F32)
    SROWS16 = sb("SROWS16", [1, 2048], BF16)
    RBUF = sb("RBUF", [128, 64], F32)
    LNB = sb("LNB", [128, 32], F32)
    CONSTS = {
        "IDENT": sb("cIDENT", [128, 128], BF16),
        "IDENT32": sb("cIDENT32", [128, 128], F32),
        "ONESROW": sb("cONESROW", [1, 128], BF16),
        "ONESCOL": sb("cONESCOL", [128, 1], F32),
        "LTRI": sb("cLTRI", [128, 128], F32),
        "R0": sb("cR0", [64, 1024], BF16),
        "C0": sb("cC0", [128, 8 * 64], BF16),
    }
    LT_ROW0 = CONSTS["LTRI"]

    ps = nc.alloc_psum_tensor
    PSW0 = ps("PSW0", [128, 1024], F32)
    PSW1 = ps("PSW1", [128, 1024], F32)
    PSA = ps("PSA", [128, 512], F32)
    PSB = ps("PSB", [128, 512], F32)
    PSC = ps("PSC", [128, 512], F32)
    PSD = ps("PSD", [128, 512], F32)
    PSW = {"0L": PSW0[:, 0:512], "0R": PSW0[:, 512:1024],
           "1L": PSW1[:, 0:512], "1R": PSW1[:, 512:1024]}

    sems = {s: nc.alloc_semaphore(s) for s in P.sem_names}

    BC = {}
    _bc = [0]

    def bc_alloc(name, k):
        BC[name] = _bc[0]
        _bc[0] += k

    ROW_OFF = {}

    def load_row16(vec_ap, slot, name):
        ROW_OFF[name] = slot * 1024
        P.dma("gpsimd", SROWS16[0:1, slot * 1024:(slot + 1) * 1024], vec_ap[None, :],
              wr=[f"ROW16{slot}"])

    def load_vec_chunks(vec_ap, n, dst_col0):
        k = n // 128
        P.dma("sync", XST[0:k, 0:128], vec_ap.rearrange("(k p) -> k p", p=128), wr=["XST"])
        P.step("tensor", lambda e, k=k: e.matmul(
            PSC[:, 0:k], XST[0:k, 0:128], CONSTS["IDENT32"][0:k, 0:k],
            start=True, stop=True), rd=["XST", "CONST"], wr=["pC"])
        P.step("vector", lambda e, k=k, dst_col0=dst_col0: e.tensor_copy(
            SBUF_BIAS[:, dst_col0:dst_col0 + k], PSC[:, 0:k]), rd=["pC"], wr=["BB"])

    def load_vec64_col(vec_ap, dst_col):
        P.dma("sync", XST[0:1, 0:64], vec_ap[None, :], wr=["XST"])
        P.step("tensor", lambda e: e.matmul(PSC[0:64, 0:1], XST[0:1, 0:64],
                                            CONSTS["ONESCOL"][0:1, 0:1], start=True, stop=True),
               rd=["XST", "CONST"], wr=["pC"])
        P.step("vector", lambda e, dst_col=dst_col: e.tensor_copy(
            SBUF_BIAS[0:64, dst_col:dst_col + 1], PSC[0:64, 0:1]), rd=["pC"], wr=["BB"])

    def rep_row_f32(vec_ap, dst_sb, dst_key):
        P.dma("sync", XST[0:1, 0:1024], vec_ap[None, :], wr=["XST"])
        for t in range(2):
            psq, pk = (PSC, "pC") if t == 0 else (PSD, "pD")
            P.step("tensor", lambda e, t=t, psq=psq: e.matmul(
                psq[:, :], LT_ROW0[0:1, :], XST[0:1, t * 512:(t + 1) * 512],
                start=True, stop=True), rd=["XST", "CONST"], wr=[pk])
        for t in range(2):
            psq, pk = (PSC, "pC") if t == 0 else (PSD, "pD")
            P.step("vector", lambda e, t=t, psq=psq: e.tensor_copy(
                dst_sb[:, t * 512:(t + 1) * 512], psq[:, :]), rd=[pk], wr=[dst_key])

    def stream_w_tiles(w3d, fchunk, slot, extra_rd=()):
        P.dma("gpsimd", WST[:, slot * 1024:(slot + 1) * 1024].rearrange("p (c f) -> p c f", f=128),
              w3d[:, :, fchunk * 128:(fchunk + 1) * 128].transpose([1, 0, 2]),
              rd=list(extra_rd), wr=[f"WST{slot}"])

    def cross_sum_11(src_col, dst_col):
        P.step("tensor", lambda e: e.matmul(PSA[0:1, 0:1], SMALL[:, src_col:src_col + 1],
                                            CONSTS["ONESCOL"][:, 0:1], start=True, stop=True),
               rd=["SM", "CONST"], wr=["pA"])
        P.step("vector", lambda e: e.tensor_copy(SMALL[0:1, dst_col:dst_col + 1], PSA[0:1, 0:1]),
               rd=["pA"], wr=["SM"])

    def rep_11(src_col, dst_col):
        P.step("tensor", lambda e: e.matmul(PSA[:, 0:1], LT_ROW0[0:1, :],
                                            SMALL[0:1, src_col:src_col + 1], start=True, stop=True),
               rd=["SM", "CONST"], wr=["pA"])
        P.step("vector", lambda e: e.tensor_copy(SMALL[:, dst_col:dst_col + 1], PSA[:, 0:1]),
               rd=["pA"], wr=["SM"])

    def swap_dmas(rt0, rt1, k0, k1):
        P.dma("sync", rt1[0:32, :], rt0[32:64, :], rd=[k0], wr=[k1])
        P.dma("sync", rt1[32:64, :], rt0[0:32, :], rd=[k0], wr=[k1])
        P.dma("sync", rt1[64:96, :], rt0[96:128, :], rd=[k0], wr=[k1])
        P.dma("sync", rt1[96:128, :], rt0[64:96, :], rd=[k0], wr=[k1])

    def phase_A(layer, src_x, xkey):
        if layer == 0:
            P.barrier()
        else:
            P.fence(["CTXe", "CTXa1", "CTXb1", "PBa", "PBb", "RT0", "RT1", "CTT",
                     "XST", "STGcd", "STGc", "STGd", "STGa", "STGb", "SM0", "SM1",
                     "SMe", "SMg", "LNB0", "LNB1", "STA0", "STA1", "STB0", "STB1", "RSL"])
        P.step("vector", lambda e: e.memset(SMALL[:, 40:41], float(EPS_LN)), wr=["SMe"])
        if layer == 0:
            rep_row_f32(W["L0_ffn_routing_token"], REPB, "REPB")
            P.step("vector", lambda e: e.reduce_sum(SMALL[:, 8:9], REPB[:, :], axis=AX.X),
                   rd=["REPB"], wr=["SMg"])
        for i in range(NT):
            par = i % 2
            o = 16 * par
            smk = f"SM{par}"
            lnk = f"LNB{par}"
            lo = 16 * par
            xt = XST[:, :] if par == 0 else STG[:, 1024:2048]
            xtk = "XST" if par == 0 else "STGcd"
            cak, cbk = f"CTXa{par}", f"CTXb{par}"
            if layer == 0:
                ctxo = AR_CTX if par == 0 else AR_CTX + 2048
                xn_tm = AR(ctxo, 1024)
                xr_tm = AR(ctxo + 1024, 1024)
                sta = AR(AR_QSL + 1024 * par, 1024)
                stb = AR(AR_QSL + 2048 + 1024 * par, 1024)
            else:
                xn_tm = AR(84992 + 1024 * par, 1024)
                xr_tm = None
                sta = AR(87040 + 1024 * par, 1024)
                stb = None
            stak = f"STA{par}"
            stbk = f"STB{par}"
            P.dma("sync", xt, src_x[i * 128:(i + 1) * 128, :],
                  rd=[xkey if layer == 0 else f"d_xb{i // 4}"], wr=[xtk])
            P.step("vector", lambda e, xt=xt, lo=lo: e.bn_stats(LNB[:, lo:lo + 6], xt[:, 0:512]),
                   rd=[xtk], wr=[lnk])
            P.step("vector", lambda e, xt=xt, lo=lo: e.bn_stats(LNB[:, lo + 6:lo + 12], xt[:, 512:1024]),
                   rd=[xtk], wr=[lnk])
            P.step("vector", lambda e, o=o, lo=lo: e.bn_aggr(SMALL[:, o:o + 2], LNB[:, lo:lo + 12]),
                   rd=[lnk], wr=[smk])
            P.step("scalar", lambda e, o=o: e.activation(SMALL[:, o + 2:o + 3], SMALL[:, o + 1:o + 2],
                                                         AF.Ln, bias=SMALL[:, 40:41]),
                   rd=[smk, "SMe"], wr=[smk])
            P.step("scalar", lambda e, o=o: e.activation(SMALL[:, o + 2:o + 3], SMALL[:, o + 2:o + 3],
                                                         AF.Exp, scale=-0.5), rd=[smk], wr=[smk])
            P.step("vector", lambda e, o=o: e.tensor_tensor(SMALL[:, o + 3:o + 4], SMALL[:, o:o + 1],
                                                            SMALL[:, o + 2:o + 3], ALU.mult),
                   rd=[smk], wr=[smk])
            P.step("vector", lambda e, o=o: e.tensor_scalar(SMALL[:, o + 3:o + 4], SMALL[:, o + 3:o + 4],
                                                            -1.0, None, ALU.mult), rd=[smk], wr=[smk])
            P.step("scalar", lambda e, o=o, xn_tm=xn_tm, xt=xt: e.activation(
                xn_tm, xt, AF.Identity, bias=SMALL[:, o + 3:o + 4], scale=SMALL[:, o + 2:o + 3]),
                rd=[xtk, smk], wr=[cak])
            if layer == 0:
                P.step("vector", lambda e, o=o: e.tensor_tensor(SMALL[:, o + 5:o + 6], SMALL[:, o + 2:o + 3], SMALL[:, o + 2:o + 3], ALU.mult), rd=[smk], wr=[smk])
                P.step("vector", lambda e, o=o: e.tensor_tensor(SMALL[:, o + 5:o + 6], SMALL[:, o + 5:o + 6], SMALL[:, o + 1:o + 2], ALU.mult), rd=[smk], wr=[smk])
                P.step("vector", lambda e, o=o: e.tensor_scalar(SMALL[:, o + 5:o + 6], SMALL[:, o + 5:o + 6], 1024.0, None, ALU.mult), rd=[smk], wr=[smk])
                P.step("scalar", lambda e, o=o: e.activation(SMALL[:, o + 5:o + 6], SMALL[:, o + 5:o + 6], AF.Ln), rd=[smk], wr=[smk])
                P.step("scalar", lambda e, o=o: e.activation(SMALL[:, o + 5:o + 6], SMALL[:, o + 5:o + 6], AF.Exp, scale=0.5), rd=[smk], wr=[smk])
                P.step("vector", lambda e, o=o: e.tensor_scalar(SMALL[:, o + 5:o + 6], SMALL[:, o + 5:o + 6], 1e-12, None, ALU.max), rd=[smk], wr=[smk])
                P.step("vector", lambda e, o=o: e.reciprocal(SMALL[:, o + 4:o + 5], SMALL[:, o + 5:o + 6]), rd=[smk], wr=[smk])
                P.step("vector", lambda e, o=o: e.tensor_scalar(SMALL[:, o + 4:o + 5], SMALL[:, o + 4:o + 5], 32.0, None, ALU.mult), rd=[smk], wr=[smk])
                P.step("vector", lambda e, o=o, xt=xt: e.scalar_tensor_tensor(
                    STG[:, 0:1024], xt, 1.0, REPB[:, :], ALU.mult, ALU.mult,
                    accum_out=SMALL[:, o + 6:o + 7]), rd=[xtk, "REPB"], wr=["STGa", "STGb", smk])
                P.step("vector", lambda e, o=o: e.tensor_tensor(SMALL[:, o + 7:o + 8], SMALL[:, o:o + 1], SMALL[:, 8:9], ALU.mult), rd=[smk, "SMg"], wr=[smk])
                P.step("vector", lambda e, o=o: e.tensor_tensor(SMALL[:, o + 7:o + 8], SMALL[:, o + 6:o + 7], SMALL[:, o + 7:o + 8], ALU.subtract), rd=[smk], wr=[smk])
                P.step("vector", lambda e, i=i, o=o: e.tensor_tensor(RBUF[:, i:i + 1], SMALL[:, o + 7:o + 8], SMALL[:, o + 2:o + 3], ALU.mult), rd=[smk], wr=["RB"])
                P.step("vector", lambda e, o=o, xr_tm=xr_tm, xn_tm=xn_tm: e.tensor_scalar(
                    xr_tm, xn_tm, SMALL[:, o + 4:o + 5], None, ALU.mult), rd=[cak, smk], wr=[cbk])
            for half in range(2):
                psq, pk = (PSC, "pC") if half == 0 else (PSD, "pD")
                for j in range(4):
                    jj = half * 4 + j
                    P.step("tensor", lambda e, jj=jj, psq=psq, j=j, xn_tm=xn_tm: e.matmul(
                        psq[:, j * 128:(j + 1) * 128], xn_tm[:, jj * 128:(jj + 1) * 128],
                        CONSTS["IDENT"][:, :], start=True, stop=True),
                        rd=[cak, "CONST"], wr=[pk])
                P.step("vector", lambda e, half=half, psq=psq, sta=sta: e.tensor_copy(
                    sta[:, half * 512:(half + 1) * 512], psq[:, 0:512]), rd=[pk], wr=[stak])
            P.dma("sync", xnT_hbm[:, :, i * 128:(i + 1) * 128].transpose([1, 0, 2]),
                  sta.rearrange("p (c n) -> p c n", n=128), rd=[stak], wr=[f"d_xnT{i // 4}"])
            if layer == 0:
                for half in range(2):
                    psq, pk = (PSC, "pC") if half == 0 else (PSD, "pD")
                    for j in range(4):
                        jj = half * 4 + j
                        P.step("tensor", lambda e, jj=jj, psq=psq, j=j, xr_tm=xr_tm: e.matmul(
                            psq[:, j * 128:(j + 1) * 128], xr_tm[:, jj * 128:(jj + 1) * 128],
                            CONSTS["IDENT"][:, :], start=True, stop=True),
                            rd=[cbk, "CONST"], wr=[pk])
                    P.step("vector", lambda e, half=half, psq=psq, stb=stb: e.tensor_copy(
                        stb[:, half * 512:(half + 1) * 512], psq[:, 0:512]), rd=[pk], wr=[stbk])
                P.dma("sync", xrT_hbm[:, :, i * 128:(i + 1) * 128].transpose([1, 0, 2]),
                      stb.rearrange("p (c n) -> p c n", n=128), rd=[stbk], wr=[f"d_xrT{i // 2}"])

    def phase_routing():
        sS = RBUF[:, 0:32]
        Kf = float(np.float32(min(NUM_HEAVY * 9.0 / 8.0, N)))
        logk = float(np.log(np.float32(Kf)).astype(np.float32))
        V = lambda fn, rd=(), wr=(): P.step("vector", fn, rd=rd, wr=wr)
        SC = lambda fn, rd=(), wr=(): P.step("scalar", fn, rd=rd, wr=wr)
        sB = RBUF[:, 32:64]
        V(lambda e: e.tensor_scalar(sB, sS, -1.0, None, ALU.mult), rd=["RB"], wr=["RBb"])
        for it in range(COOR_ITERS):
            if it == 0:
                V(lambda e: e.tensor_tensor(STG[:, 0:32], sS, sB, ALU.add),
                  rd=["RB", "RBb"], wr=["STGa"])
            else:
                V(lambda e: e.tensor_scalar(STG[:, 32:64], sS, SMALL[:, 13:14], None, ALU.add),
                  rd=["RB", "SM"], wr=["STGa"])
                V(lambda e: e.tensor_scalar(sB, STG[:, 32:64], 0.0, -1.0, ALU.max, ALU.mult),
                  rd=["STGa"], wr=["RBb"])
                V(lambda e: e.tensor_tensor(STG[:, 0:32], sS, sB, ALU.add),
                  rd=["RB", "RBb"], wr=["STGa"])
            V(lambda e: e.reduce_max(SMALL[:, 16:17], STG[:, 0:32], axis=AX.X),
              rd=["STGa"], wr=["SM"])
            P.step("tensor", lambda e: e.matmul(PSA[0:1, 0:128], SMALL[:, 16:17],
                                                CONSTS["IDENT32"][:, :], start=True, stop=True),
                   rd=["SM", "CONST"], wr=["pA"])
            V(lambda e: e.tensor_copy(STG[0:1, 128:256], PSA[0:1, 0:128]), rd=["pA"], wr=["STGu"])
            V(lambda e: e.reduce_max(SMALL[0:1, 17:18], STG[0:1, 128:256], axis=AX.X),
              rd=["STGu"], wr=["SM"])
            rep_11(17, 18)
            V(lambda e: e.tensor_scalar(SMALL[:, 19:20], SMALL[:, 18:19], -1.0, None, ALU.mult),
              rd=["SM"], wr=["SM"])
            SC(lambda e: e.activation(STG[:, 32:64], STG[:, 0:32], AF.Exp,
                                      bias=SMALL[:, 19:20], accum_out=SMALL[:, 10:11]),
               rd=["STGa", "SM"], wr=["STGt", "SM"])
            cross_sum_11(10, 11)
            SC(lambda e: e.activation(SMALL[0:1, 14:15], SMALL[0:1, 11:12], AF.Ln),
               rd=["SM"], wr=["SM"])
            V(lambda e: e.tensor_tensor(SMALL[0:1, 14:15], SMALL[0:1, 14:15],
                                        SMALL[0:1, 17:18], ALU.add), rd=["SM"], wr=["SM"])
            V(lambda e: e.tensor_scalar(SMALL[0:1, 12:13], SMALL[0:1, 14:15],
                                        logk, -1.0, ALU.subtract, ALU.mult), rd=["SM"], wr=["SM"])
            rep_11(12, 13)
        V(lambda e: e.tensor_scalar(STG[:, 0:32], sS, SMALL[:, 13:14], None, ALU.add),
          rd=["RB", "SM"], wr=["STGa"])
        V(lambda e: e.tensor_scalar(STG[:, 32:64], STG[:, 0:32], 0.0, -1.0, ALU.max, ALU.mult),
          rd=["STGa"], wr=["STGa"])
        V(lambda e: e.tensor_tensor(STG[:, 0:32], STG[:, 0:32], STG[:, 32:64], ALU.add),
          rd=["STGa"], wr=["STGa"])
        SC(lambda e: e.activation(STG[:, 64:96], STG[:, 0:32], AF.Exp), rd=["STGa"], wr=["STGs"])
        scores = STG[:, 64:96]
        V(lambda e: e.memset(SMALL[0:1, 20:21], 0.0), rd=["SM"], wr=["SM"])
        V(lambda e: e.memset(SMALL[0:1, 21:22], 1.5), rd=["SM"], wr=["SM"])
        for it in range(BISECT_ITERS):
            V(lambda e: e.tensor_tensor(SMALL[0:1, 22:23], SMALL[0:1, 20:21],
                                        SMALL[0:1, 21:22], ALU.add), rd=["SM"], wr=["SM"])
            V(lambda e: e.tensor_scalar(SMALL[0:1, 22:23], SMALL[0:1, 22:23],
                                        0.5, None, ALU.mult), rd=["SM"], wr=["SM"])
            rep_11(22, 23)
            V(lambda e: e.tensor_scalar(STG[:, 96:128], scores, SMALL[:, 23:24],
                                        0.0, ALU.is_ge, ALU.add, accum_out=SMALL[:, 10:11]),
              rd=["STGs", "SM"], wr=["STGt", "SM"])
            cross_sum_11(10, 24)
            V(lambda e: e.tensor_scalar(SMALL[0:1, 25:26], SMALL[0:1, 24:25],
                                        float(NUM_HEAVY), None, ALU.is_ge), rd=["SM"], wr=["SM"])
            V(lambda e: e.tensor_tensor(SMALL[0:1, 26:27], SMALL[0:1, 22:23],
                                        SMALL[0:1, 20:21], ALU.subtract), rd=["SM"], wr=["SM"])
            V(lambda e: e.tensor_tensor(SMALL[0:1, 26:27], SMALL[0:1, 26:27],
                                        SMALL[0:1, 25:26], ALU.mult), rd=["SM"], wr=["SM"])
            V(lambda e: e.tensor_tensor(SMALL[0:1, 20:21], SMALL[0:1, 20:21],
                                        SMALL[0:1, 26:27], ALU.add), rd=["SM"], wr=["SM"])
            V(lambda e: e.tensor_scalar(SMALL[0:1, 27:28], SMALL[0:1, 25:26],
                                        1.0, -1.0, ALU.subtract, ALU.mult), rd=["SM"], wr=["SM"])
            V(lambda e: e.tensor_tensor(SMALL[0:1, 26:27], SMALL[0:1, 22:23],
                                        SMALL[0:1, 21:22], ALU.subtract), rd=["SM"], wr=["SM"])
            V(lambda e: e.tensor_tensor(SMALL[0:1, 26:27], SMALL[0:1, 26:27],
                                        SMALL[0:1, 27:28], ALU.mult), rd=["SM"], wr=["SM"])
            V(lambda e: e.tensor_tensor(SMALL[0:1, 21:22], SMALL[0:1, 21:22],
                                        SMALL[0:1, 26:27], ALU.add), rd=["SM"], wr=["SM"])
        rep_11(21, 23)
        V(lambda e: e.tensor_scalar(STG[:, 96:128], scores, SMALL[:, 23:24],
                                    None, ALU.is_lt), rd=["STGs", "SM"], wr=["STGt"])
        V(lambda e: e.tensor_tensor(STG[:, 96:128], STG[:, 96:128], scores, ALU.mult),
          rd=["STGt", "STGs"], wr=["STGt"])
        V(lambda e: e.reduce_max(SMALL[:, 10:11], STG[:, 96:128], axis=AX.X), rd=["STGt"], wr=["SM"])
        P.step("tensor", lambda e: e.matmul(PSA[0:1, 0:128], SMALL[:, 10:11],
                                            CONSTS["IDENT32"][:, :], start=True, stop=True),
               rd=["SM", "CONST"], wr=["pA"])
        V(lambda e: e.tensor_copy(STG[0:1, 128:256], PSA[0:1, 0:128]), rd=["pA"], wr=["STGu"])
        V(lambda e: e.reduce_max(SMALL[0:1, 24:25], STG[0:1, 128:256], axis=AX.X),
          rd=["STGu"], wr=["SM"])
        rep_11(24, 25)
        V(lambda e: e.tensor_scalar(STG[:, 96:128], scores, SMALL[:, 25:26],
                                    0.0, ALU.is_gt, ALU.add, accum_out=SMALL[:, 10:11]),
          rd=["STGs", "SM"], wr=["STGt", "SM"])
        V(lambda e: e.tensor_scalar(STG[:, 128:160], scores, SMALL[:, 25:26],
                                    None, ALU.is_equal), rd=["STGs", "SM"], wr=["STGu"])
        cross_sum_11(10, 26)
        V(lambda e: e.tensor_scalar(SMALL[0:1, 27:28], SMALL[0:1, 26:27],
                                    float(NUM_HEAVY), -1.0, ALU.subtract, ALU.mult),
          rd=["SM"], wr=["SM"])
        rep_11(27, 28)
        P.step("tensor", lambda e: e.matmul(PSA[:, 0:32], CONSTS["LTRI"][:, :],
                                            STG[:, 128:160], start=True, stop=True),
               rd=["STGu", "CONST"], wr=["pA"])
        V(lambda e: e.tensor_copy(STG[:, 160:192], PSA[:, 0:32]), rd=["pA"], wr=["STGv"])
        P.step("tensor", lambda e: e.matmul(PSB[0:32, 0:1], STG[:, 128:160],
                                            CONSTS["ONESCOL"][:, 0:1], start=True, stop=True),
               rd=["STGu", "CONST"], wr=["pB"])
        V(lambda e: e.tensor_copy(SMALL[0:32, 30:31], PSB[0:32, 0:1]), rd=["pB"], wr=["SM"])
        P.step("tensor", lambda e: e.matmul(PSB[0:32, 1:2], CONSTS["LTRI"][0:32, 0:32],
                                            SMALL[0:32, 30:31], start=True, stop=True),
               rd=["SM", "CONST"], wr=["pB"])
        V(lambda e: e.tensor_copy(SMALL[0:32, 31:32], PSB[0:32, 1:2]), rd=["pB"], wr=["SM"])
        V(lambda e: e.tensor_tensor(SMALL[0:32, 31:32], SMALL[0:32, 31:32],
                                    SMALL[0:32, 30:31], ALU.subtract), rd=["SM"], wr=["SM"])
        P.step("tensor", lambda e: e.matmul(PSB[0:1, 0:32], SMALL[0:32, 31:32],
                                            CONSTS["IDENT32"][0:32, 0:32], start=True, stop=True),
               rd=["SM", "CONST"], wr=["pB"])
        V(lambda e: e.tensor_copy(STG[0:1, 192:224], PSB[0:1, 0:32]), rd=["pB"], wr=["STGw"])
        P.step("tensor", lambda e: e.matmul(PSB[:, 0:32], LT_ROW0[0:1, :],
                                            STG[0:1, 192:224], start=True, stop=True),
               rd=["STGw", "CONST"], wr=["pB"])
        V(lambda e: e.tensor_copy(STG[:, 192:224], PSB[:, 0:32]), rd=["pB"], wr=["STGw"])
        V(lambda e: e.tensor_tensor(STG[:, 160:192], STG[:, 160:192],
                                    STG[:, 192:224], ALU.add), rd=["STGv", "STGw"], wr=["STGv"])
        V(lambda e: e.tensor_scalar(STG[:, 160:192], STG[:, 160:192],
                                    SMALL[:, 28:29], None, ALU.is_le), rd=["STGv", "SM"], wr=["STGv"])
        V(lambda e: e.tensor_tensor(STG[:, 160:192], STG[:, 160:192],
                                    STG[:, 128:160], ALU.mult), rd=["STGv", "STGu"], wr=["STGv"])
        V(lambda e: e.tensor_tensor(RBUF[:, 32:64], STG[:, 96:128],
                                    STG[:, 160:192], ALU.add), rd=["STGt", "STGv"], wr=["RB"])

    def gen_KV(layer, wk3d, bk_col, rope, vb_row=0):
        if layer == 1:
            P.barrier()
        P.step("gpsimd", lambda e: e.memset(
            AR(AR_V, 33280).bitcast(FP16).rearrange("p (g x) -> p g x", x=65)[:, :, 64:65], 1.0),
            wr=["V"])
        xsl = AR(AR_XSL, 4096).rearrange("p (c n) -> p c n", n=QB)
        Kb = AR(AR_K, 32768).rearrange("p (c n) -> p c n", n=N)
        Vv = AR(AR_V, 33280).bitcast(FP16).rearrange("p (t g) -> p t g", g=1040)
        rsl_c = AR(AR_RSL, 512)
        rsl_s = AR(AR_RSL + 512, 512)
        rpair = [(AR(AR_RT0, 512), AR(AR_RT1, 512), "RT0", "RT1"),
                 (AR(AR_PB, 512), AR(AR_PB + 512, 512), "PBa", "PBb")]
        for nb in range(NQB):
            P.dma("sync", xsl, xnT_hbm[:, :, nb * QB:(nb + 1) * QB].transpose([1, 0, 2]),
                  rd=[f"d_xnT{nb}"], wr=["XSL"])
            if rope:
                P.dma("sync", rsl_c, CONST_IN["COS128"][:, nb * QB:(nb + 1) * QB], wr=["RSL"])
                P.dma("sync", rsl_s, CONST_IN["SINSIG"][:, nb * QB:(nb + 1) * QB], wr=["RSL"])
            for c in range(NCH):
                slot = c % 2
                rt0, rt1, kt0, kt1 = rpair[slot]
                psq, pk = (PSW["1L"], "pW1L") if slot == 0 else (PSW["1R"], "pW1R")
                stream_w_tiles(wk3d, c, slot)
                wt = WST[:, slot * 1024:(slot + 1) * 1024].rearrange("p (k f) -> p k f", f=128)
                for k in range(NCH):
                    P.step("tensor", lambda e, k=k, wt=wt, psq=psq: e.matmul(
                        psq, wt[:, k, :], xsl[:, k, :],
                        start=(k == 0), stop=(k == NCH - 1)),
                        rd=[f"WST{slot}", "XSL"], wr=[pk])
                P.step("vector", lambda e, c=c, psq=psq, rt0=rt0: e.tensor_scalar(
                    rt0, psq, SBUF_BIAS[:, bk_col + c:bk_col + c + 1], None, ALU.add),
                    rd=[pk, "BB"], wr=[kt0])
                if rope:
                    swap_dmas(rt0, rt1, kt0, kt1)
                    P.step("vector", lambda e, rt0=rt0: e.tensor_tensor(rt0, rt0, rsl_c, ALU.mult),
                           rd=[kt0, "RSL"], wr=[kt0])
                    P.step("vector", lambda e, rt1=rt1: e.tensor_tensor(rt1, rt1, rsl_s, ALU.mult),
                           rd=[kt1, "RSL"], wr=[kt1])
                    P.step("vector", lambda e, c=c, nb=nb, rt0=rt0, rt1=rt1: e.tensor_tensor(
                        Kb[:, c, nb * QB:(nb + 1) * QB], rt0, rt1, ALU.add),
                        rd=[kt0, kt1], wr=["K"])
                else:
                    P.step("vector", lambda e, c=c, nb=nb, rt0=rt0: e.tensor_copy(
                        Kb[:, c, nb * QB:(nb + 1) * QB], rt0), rd=[kt0], wr=["K"])
            for j in range(QB // 128):
                i = nb * (QB // 128) + j
                for half in range(2):
                    psq, pk = (PSW["0L"], "pW0L") if half == 0 else (PSW["0R"], "pW0R")
                    for k in range(NCH):
                        P.step("tensor", lambda e, j=j, k=k, psq=psq, half=half: e.matmul(
                            psq, xsl[:, k, j * 128:(j + 1) * 128],
                            AR(AR_WV, 8192)[:, k * 1024 + half * 512:k * 1024 + half * 512 + 512],
                            start=(k == 0), stop=False),
                            rd=["XSL", "WV"], wr=[pk])
                    P.step("tensor", lambda e, half=half, psq=psq: e.matmul(
                        psq, CONSTS["ONESROW"][0:1, :],
                        SROWS16[0:1, vb_row + half * 512:vb_row + (half + 1) * 512],
                        start=False, stop=True), rd=["ROW160", "CONST"], wr=[pk])
                    P.step("vector", lambda e, i=i, half=half, psq=psq: e.tensor_copy(
                        Vv[:, i, :].rearrange("p (h x) -> p h x", x=65)[:, half * 8:half * 8 + 8, 0:64],
                        psq.rearrange("p (h x) -> p h x", x=64)), rd=[pk], wr=["V"])

    def attn_qblock(layer, qb, wq3d, bq_col, rope, wo_b_row, src_x, srckey, dst_x, dstkey):
        xsl = AR(AR_XSL, 4096).rearrange("p (c n) -> p c n", n=QB)
        qsl = AR(AR_QSL, 4096).rearrange("p (c n) -> p c n", n=QB)
        rsl_c = AR(AR_RSL, 512)
        rsl_s = AR(AR_RSL + 512, 512)
        rt0, rt1 = AR(AR_RT0, 512), AR(AR_RT1, 512)
        P.dma("sync", xsl, xnT_hbm[:, :, qb * QB:(qb + 1) * QB].transpose([1, 0, 2]),
              rd=[f"d_xnT{qb}"], wr=["XSL"])
        if rope:
            P.dma("sync", rsl_c, CONST_IN["COS128"][:, qb * QB:(qb + 1) * QB], wr=["RSL"])
            P.dma("sync", rsl_s, CONST_IN["SINSIG"][:, qb * QB:(qb + 1) * QB], wr=["RSL"])
        for c in range(NCH):
            slot = c % 2
            psq, pk = (PSC, "pC") if slot == 0 else (PSD, "pD")
            stream_w_tiles(wq3d, c, slot)
            wt = WST[:, slot * 1024:(slot + 1) * 1024].rearrange("p (k f) -> p k f", f=128)
            for k in range(NCH):
                P.step("tensor", lambda e, k=k, wt=wt, psq=psq: e.matmul(
                    psq[:, :], wt[:, k, :], xsl[:, k, :],
                    start=(k == 0), stop=(k == NCH - 1)), rd=[f"WST{slot}", "XSL"], wr=[pk])
            P.step("vector", lambda e, c=c, psq=psq: e.tensor_scalar(
                rt0, psq[:, :], SBUF_BIAS[:, bq_col + c:bq_col + c + 1], None, ALU.add),
                rd=[pk, "BB"], wr=["RT0"])
            if rope:
                swap_dmas(rt0, rt1, "RT0", "RT1")
                P.step("vector", lambda e: e.tensor_tensor(rt0, rt0, rsl_c, ALU.mult),
                       rd=["RT0", "RSL"], wr=["RT0"])
                P.step("vector", lambda e: e.tensor_tensor(rt1, rt1, rsl_s, ALU.mult),
                       rd=["RT1", "RSL"], wr=["RT1"])
                P.step("vector", lambda e, c=c: e.tensor_tensor(qsl[:, c, :], rt0, rt1, ALU.add),
                       rd=["RT0", "RT1"], wr=["QSL"])
            else:
                P.step("vector", lambda e, c=c: e.tensor_copy(qsl[:, c, :], rt0),
                       rd=["RT0"], wr=["QSL"])
        Kb = AR(AR_K, 32768).rearrange("p (c n) -> p c n", n=N)
        Vv = AR(AR_V, 33280).bitcast(FP16).rearrange("p (t g) -> p t g", g=1040)
        PBs = [(AR(AR_PB, 1024).bitcast(FP16), ["PBa", "PBb"]),
               (AR(AR_RT0, 1024).bitcast(FP16), ["RT0", "RT1"]),
               (AR(AR_RSL, 1024).bitcast(FP16), ["RSL"])]
        ctx = AR(AR_CTX, 4096)
        ctxT = AR(AR_CTXT, 512)
        kt_max = (qb + 1) * (QB // 128) if layer == 0 else NT

        def emit_S(p, kt):
            par = kt % 2
            SW, swL, swR = (PSW0, "pW0L", "pW0R") if par == 0 else (PSW1, "pW1L", "pW1R")
            P.step("tensor", lambda e, p=p, kt=kt, SW=SW: e.matmul(
                SW[:, 0:512], Kb[0:64, p, kt * 128:(kt + 1) * 128], qsl[0:64, p, :],
                start=True, stop=True, tile_position=(0, 0)), rd=["K", "QSL"], wr=[swL])
            P.step("tensor", lambda e, p=p, kt=kt, SW=SW: e.matmul(
                SW[:, 512:1024], Kb[64:128, p, kt * 128:(kt + 1) * 128], qsl[64:128, p, :],
                start=True, stop=True, tile_position=(64, 0)), rd=["K", "QSL"], wr=[swR])

        for p in range(NCH):
            emit_S(p, 0)
            for kt in range(kt_max):
                SW, swL, swR = (PSW0, "pW0L", "pW0R") if kt % 2 == 0 else (PSW1, "pW1L", "pW1R")
                PB, pbk = PBs[kt % 3]
                if kt + 1 < kt_max:
                    emit_S(p, kt + 1)
                elif p + 1 < NCH:
                    pass
                P.step("scalar", lambda e, SW=SW, PB=PB: e.activation(
                    PB[:, :], SW[:, :], AF.Exp, scale=0.125), rd=[swL, swR], wr=pbk)
                if layer == 0 and kt >= qb * 4:
                    jb = kt - qb * 4
                    for h in range(2):
                        P.step("gpsimd", lambda e, h=h, jb=jb, PB=PB: e.affine_select(
                            PB[:, h * 512:(h + 1) * 512], PB[:, h * 512:(h + 1) * 512],
                            [[1, 512]], ALU.is_ge, 0.0, base=-jb * 128, channel_multiplier=-1),
                            rd=pbk, wr=pbk)
                if layer == 1 and qb == 0 and kt == NT - 1:
                    for h in range(2):
                        P.step("gpsimd", lambda e, h=h, PB=PB: e.affine_select(
                            PB[:, h * 512:(h + 1) * 512], PB[:, h * 512:(h + 1) * 512],
                            [[1, 512]], ALU.is_gt, 0.0, base=127, channel_multiplier=-1),
                            rd=pbk, wr=pbk)
                if layer == 1 and qb == NQB - 1 and kt == 0:
                    for h in range(2):
                        P.step("gpsimd", lambda e, h=h, PB=PB: e.affine_select(
                            PB[:, h * 512:(h + 1) * 512], PB[:, h * 512:(h + 1) * 512],
                            [[-1, 512]], ALU.is_gt, 0.0, base=511, channel_multiplier=1),
                            rd=pbk, wr=pbk)
                P.step("tensor", lambda e, p=p, kt=kt, kt_max=kt_max, PB=PB: e.matmul(
                    PSA[0:65, :], Vv[:, kt, 2 * p * 65:2 * p * 65 + 65], PB[:, 0:512],
                    start=(kt == 0), stop=(kt == kt_max - 1)), rd=["V"] + pbk, wr=["pA"])
                P.step("tensor", lambda e, p=p, kt=kt, kt_max=kt_max, PB=PB: e.matmul(
                    PSB[0:65, :], Vv[:, kt, (2 * p + 1) * 65:(2 * p + 1) * 65 + 65], PB[:, 512:1024],
                    start=(kt == 0), stop=(kt == kt_max - 1)), rd=["V"] + pbk, wr=["pB"])
            P.step("vector", lambda e: e.reciprocal(REPB[64:65, 0:512], PSA[64:65, :]),
                   rd=["pA"], wr=["REPB"])
            P.step("vector", lambda e: e.reciprocal(REPB[64:65, 512:1024], PSB[64:65, :]),
                   rd=["pB"], wr=["REPB"])
            P.step("tensor", lambda e: e.matmul(PSC[0:64, :], CONSTS["LTRI"][64:65, 64:128],
                                                REPB[64:65, 0:512], start=True, stop=True),
                   rd=["REPB", "CONST"], wr=["pC"])
            P.step("tensor", lambda e: e.matmul(PSD[0:64, :], CONSTS["LTRI"][64:65, 64:128],
                                                REPB[64:65, 512:1024], start=True, stop=True),
                   rd=["REPB", "CONST"], wr=["pD"])
            P.step("vector", lambda e: e.tensor_copy(STG[0:64, 1024:1536], PSC[0:64, :]),
                   rd=["pC"], wr=["STGc"])
            P.step("vector", lambda e: e.tensor_copy(STG[0:64, 1536:2048], PSD[0:64, :]),
                   rd=["pD"], wr=["STGd"])
            P.step("vector", lambda e, p=p: e.tensor_tensor(
                ctx[0:64, p * 512:(p + 1) * 512], PSA[0:64, :], STG[0:64, 1024:1536], ALU.mult),
                rd=["pA", "STGc"], wr=["CTXe"])
            P.step("vector", lambda e: e.tensor_tensor(
                ctxT[0:64, :], PSB[0:64, :], STG[0:64, 1536:2048], ALU.mult),
                rd=["pB", "STGd"], wr=["CTT"])
            P.dma("sync", ctx[64:128, p * 512:(p + 1) * 512], ctxT[0:64, :],
                  rd=["CTT"], wr=["CTXe"])
        Ct = ctx.rearrange("p (pp n) -> p pp n", n=512)
        WOr = AR(AR_WV, 8192)
        for j in range(QB // 128):
            ti = qb * (QB // 128) + j
            for half in range(2):
                psq, pk = (PSW["0L"], "pW0L") if half == 0 else (PSW["0R"], "pW0R")
                for p in range(NCH):
                    P.step("tensor", lambda e, j=j, p=p, half=half, psq=psq: e.matmul(
                        psq, Ct[:, p, j * 128:(j + 1) * 128],
                        WOr[:, p * 1024 + half * 512:p * 1024 + half * 512 + 512],
                        start=(p == 0), stop=(p == NCH - 1 and wo_b_row is None)),
                        rd=["CTXe", "WV"], wr=[pk])
                if wo_b_row is not None:
                    P.step("tensor", lambda e, half=half, psq=psq: e.matmul(
                        psq, CONSTS["ONESROW"][0:1, :],
                        SROWS16[0:1, wo_b_row + half * 512:wo_b_row + (half + 1) * 512],
                        start=False, stop=True), rd=["ROW160", "CONST"], wr=[pk])
            P.dma("sync", XST[:, :], src_x[ti * 128:(ti + 1) * 128, :],
                  rd=[srckey if srckey != "d_xb" else f"d_xb{ti // 4}"], wr=["XST"])
            P.step("vector", lambda e: e.tensor_tensor(STG[:, 0:512], XST[:, 0:512], PSW["0L"],
                                                       ALU.add), rd=["XST", "pW0L"], wr=["STGa"])
            P.step("vector", lambda e: e.tensor_tensor(STG[:, 512:1024], XST[:, 512:1024], PSW["0R"],
                                                       ALU.add), rd=["XST", "pW0R"], wr=["STGb"])
            P.dma("sync", dst_x[ti * 128:(ti + 1) * 128, :], STG[:, 0:1024],
                  rd=["STGa", "STGb"], wr=[dstkey])

    def ffn_l0():
        P.dma("gpsimd", AR(AR_W1H, 32768).rearrange("p (c f) -> p c f", f=DH),
              W["L0_ffn_heavy_w1"].rearrange("(c p) d -> c p d", p=128).transpose([1, 0, 2]),
              wr=["W1H", "K", "T0", "T1"])
        for c in range(NCH):
            P.step("vector", lambda e, c=c: e.tensor_scalar(
                AR(AR_W1H + c * DH, DH), AR(AR_W1H + c * DH, DH),
                SBUF_BIAS[:, BC["hgam"] + c:BC["hgam"] + c + 1], None, ALU.mult),
                rd=["W1H", "BB"], wr=["W1H"])
        P.dma("gpsimd", AR(AR_W2H, 32768).rearrange("p (c f) -> p c f", f=1024),
              W["L0_ffn_heavy_w2"].rearrange("(c p) d -> c p d", p=128).transpose([1, 0, 2]),
              wr=["W2H", "V", "WV"])
        P.dma("gpsimd", AR(AR_LW1, 4096).rearrange("p (c f) -> p c f", f=DL),
              W["L0_ffn_light_w1"].rearrange("(c p) d -> c p d", p=128).transpose([1, 0, 2]),
              rd=["QSL", "XSL"], wr=["LW1"])
        for c in range(NCH):
            P.step("vector", lambda e, c=c: e.tensor_scalar(
                AR(AR_LW1 + c * DL, DL), AR(AR_LW1 + c * DL, DL),
                SBUF_BIAS[:, BC["lgam"] + c:BC["lgam"] + c + 1], None, ALU.mult),
                rd=["LW1", "BB"], wr=["LW1"])
        P.dma("gpsimd", AR(AR_LW2, 4096).rearrange("p (c f) -> p c f", f=1024),
              W["L0_ffn_light_w2"].rearrange("(c p) d -> c p d", p=128).transpose([1, 0, 2]),
              rd=["XSL"], wr=["LW2", "CTXe", "CTXa0", "CTXa1", "CTXb0", "CTXb1"])
        w1t = AR(AR_W1H, 32768).rearrange("p (c f) -> p c f", f=DH)
        w2c = AR(AR_W2H, 32768).rearrange("p (c f) -> p c f", f=1024)
        lw1 = AR(AR_LW1, 4096).rearrange("p (c f) -> p c f", f=DL)
        lw2 = AR(AR_LW2, 4096).rearrange("p (c f) -> p c f", f=1024)
        Hc = AR(AR_HB, 8192).rearrange("p (c n) -> p c n", n=FB)
        LHc = AR(AR_LHB, 1024).rearrange("p (c n) -> p c n", n=FB)
        xr = AR(AR_XRB, 2048).rearrange("p (c n) -> p c n", n=FB)
        hps = [(PSW["0L"], "pW0L"), (PSW["0R"], "pW0R"), (PSW["1L"], "pW1L"), (PSW["1R"], "pW1R")]
        for fb in range(NFB):
            P.dma("sync", xr, xrT_hbm[:, :, fb * FB:(fb + 1) * FB].transpose([1, 0, 2]),
                  rd=[f"d_xrT{fb}", "QSL"], wr=["XRB"])
            for dh in range(DH // 128):
                psq, pk = hps[dh % 4]
                hk = f"HB{dh % 2}"
                for k in range(NCH):
                    P.step("tensor", lambda e, dh=dh, k=k, psq=psq: e.matmul(
                        psq[:, 0:FB], w1t[:, k, dh * 128:(dh + 1) * 128], xr[:, k, :],
                        start=(k == 0), stop=(k == NCH - 1)), rd=["W1H", "XRB"], wr=[pk])
                P.step("scalar", lambda e, dh=dh, psq=psq: e.activation(
                    Hc[:, dh, :], psq[:, 0:FB], AF.Gelu,
                    bias=SBUF_BIAS[:, BC["hb1"] + dh:BC["hb1"] + dh + 1]),
                    rd=[pk, "BB", "V"], wr=[hk])
            for dl in range(DL // 128):
                psq, pk = hps[dl % 4]
                for k in range(NCH):
                    P.step("tensor", lambda e, dl=dl, k=k, psq=psq: e.matmul(
                        psq[:, 0:FB], lw1[:, k, dl * 128:(dl + 1) * 128], xr[:, k, :],
                        start=(k == 0), stop=(k == NCH - 1)), rd=["LW1", "XRB"], wr=[pk])
                P.step("scalar", lambda e, dl=dl, psq=psq: e.activation(
                    LHc[:, dl, :], psq[:, 0:FB], AF.Gelu,
                    bias=SBUF_BIAS[:, BC["lb1"] + dl:BC["lb1"] + dl + 1]),
                    rd=[pk, "BB", "V", "QSL"], wr=["LHB"])
            for j in range(FB // 128):
                ti = fb * (FB // 128) + j
                for half in range(2):
                    psq, pk = (PSC, "pC") if half == 0 else (PSD, "pD")
                    psh, phk = (PSA, "pA") if half == 0 else (PSB, "pB")
                    for dl in range(DL // 128):
                        P.step("tensor", lambda e, dl=dl, j=j, half=half, psq=psq: e.matmul(
                            psq[:, :], LHc[:, dl, j * 128:(j + 1) * 128],
                            lw2[:, dl, half * 512:(half + 1) * 512],
                            start=(dl == 0), stop=False), rd=["LHB", "LW2"], wr=[pk])
                    P.step("tensor", lambda e, half=half, psq=psq: e.matmul(
                        psq[:, :], CONSTS["ONESROW"][0:1, :],
                        SROWS16[0:1, ROW_OFF["lb2"] + half * 512:ROW_OFF["lb2"] + (half + 1) * 512],
                        start=False, stop=True), rd=["ROW160", "CONST"], wr=[pk])
                    for dh in range(DH // 128):
                        P.step("tensor", lambda e, dh=dh, j=j, half=half, psh=psh: e.matmul(
                            psh[:, :], Hc[:, dh, j * 128:(j + 1) * 128],
                            w2c[:, dh, half * 512:(half + 1) * 512],
                            start=(dh == 0), stop=False), rd=[f"HB{dh % 2}", "W2H"], wr=[phk])
                    P.step("tensor", lambda e, half=half, psh=psh: e.matmul(
                        psh[:, :], CONSTS["ONESROW"][0:1, :],
                        SROWS16[0:1, ROW_OFF["hb2"] + half * 512:ROW_OFF["hb2"] + (half + 1) * 512],
                        start=False, stop=True), rd=["ROW161", "CONST"], wr=[phk])
                P.dma("sync", XST[:, :], xprime[ti * 128:(ti + 1) * 128, :],
                      rd=["d_xp"], wr=["XST"])
                P.step("vector", lambda e: e.tensor_tensor(STG[:, 0:512], XST[:, 0:512], PSC[:, :],
                                                           ALU.add), rd=["XST", "pC"], wr=["STGa"])
                P.step("vector", lambda e: e.tensor_tensor(STG[:, 512:1024], XST[:, 512:1024],
                                                           PSD[:, :], ALU.add),
                       rd=["XST", "pD"], wr=["STGb"])
                P.step("vector", lambda e, ti=ti: e.scalar_tensor_tensor(
                    STG[:, 0:512], PSA[:, :], RBUF[:, 32 + ti:33 + ti], STG[:, 0:512],
                    ALU.mult, ALU.add), rd=["pA", "RB", "STGa"], wr=["STGa"])
                P.step("vector", lambda e, ti=ti: e.scalar_tensor_tensor(
                    STG[:, 512:1024], PSB[:, :], RBUF[:, 32 + ti:33 + ti], STG[:, 512:1024],
                    ALU.mult, ALU.add), rd=["pB", "RB", "STGb"], wr=["STGb"])
                P.dma("sync", x_buf[ti * 128:(ti + 1) * 128, :], STG[:, 0:1024],
                      rd=["STGa", "STGb"], wr=[f"d_xb{ti // 4}"])

    def transpose_l1_weights():
        t0 = AR(0, 1024)
        t1 = AR(1024, 1024)
        for nm, src in [("L1q", W["L1_attn_Wq"]), ("L1k", W["L1_attn_Wk"]),
                        ("L1v", W["L1_attn_Wv"]), ("L1o", W["L1_attn_Wo"])]:
            for fc in range(NCH):
                P.dma("gpsimd", t0.rearrange("p (c f) -> p c f", f=128),
                      src.rearrange("(c p) (k f) -> c p k f", p=128, f=128)[fc],
                      wr=["T0", "K"])
                for k in range(NCH):
                    psq, pk = (PSC, "pC") if k % 2 == 0 else (PSD, "pD")
                    P.step("tensor", lambda e, k=k, psq=psq: e.matmul(
                        psq[:, 0:128], t0[:, k * 128:(k + 1) * 128],
                        CONSTS["IDENT"][:, :], start=True, stop=True),
                        rd=["T0", "CONST"], wr=[pk])
                    P.step("vector", lambda e, k=k, psq=psq: e.tensor_copy(
                        t1[:, k * 128:(k + 1) * 128], psq[:, 0:128]), rd=[pk], wr=["T1", "K"])
                P.dma("sync", wT_hbm[nm][:, :, fc * 128:(fc + 1) * 128].transpose([1, 0, 2]),
                      t1.rearrange("p (k f) -> p k f", f=128), rd=["T1"], wr=["d_wT"])

    def ffn_l1():
        P.barrier()
        P.dma("gpsimd", AR(AR_WIN, 512).rearrange("p (c f) -> p c f", f=64),
              W["L1_ffn_Win"].rearrange("(c p) m -> c p m", p=128).transpose([1, 0, 2]),
              wr=["WIN"])
        P.dma("gpsimd", AR(AR_WOUT, 1024)[0:64, :], W["L1_ffn_Wout"][:, :], wr=["WOU"])
        win = AR(AR_WIN, 512).rearrange("p (c f) -> p c f", f=64)
        wout = AR(AR_WOUT, 1024)
        xsl = AR(AR_XSL, 4096).rearrange("p (c n) -> p c n", n=QB)
        sbf = AR(AR_SBF, 512)
        hf = AR(AR_HF, 4096)
        gbf = AR(AR_GBF, 512)
        for qb in range(NQB):
            P.dma("sync", xsl, xnT_hbm[:, :, qb * QB:(qb + 1) * QB].transpose([1, 0, 2]),
                  rd=[f"d_xnT{qb}"], wr=["XSL"])
            for k in range(NCH):
                P.step("tensor", lambda e, k=k: e.matmul(
                    PSC[0:64, :], win[:, k, :], xsl[:, k, :],
                    start=(k == 0), stop=(k == NCH - 1)), rd=["WIN", "XSL"], wr=["pC"])
            P.step("vector", lambda e: e.tensor_scalar(sbf[0:64, :], PSC[0:64, :],
                                                       SBUF_BIAS[0:64, BC["bin"]:BC["bin"] + 1],
                                                       None, ALU.add),
                   rd=["pC", "BB"], wr=["SBF"])
            for c in range(NCH):
                P.step("tensor", lambda e, c=c: e.matmul(
                    PSD[:, :], CONSTS["R0"][0:64, c * 128:(c + 1) * 128], sbf[0:64, :],
                    start=True, stop=True), rd=["CONST", "SBF"], wr=["pD"])
                P.step("scalar", lambda e, c=c: e.activation(
                    hf[:, c * 512:(c + 1) * 512], PSD[:, :], AF.Relu,
                    bias=SBUF_BIAS[:, BC["b1f"] + c:BC["b1f"] + c + 1],
                    scale=SBUF_BIAS[:, BC["w1f"] + c:BC["w1f"] + c + 1]),
                    rd=["pD", "BB"], wr=["HF"])
                P.step("vector", lambda e, c=c: e.tensor_scalar(
                    hf[:, c * 512:(c + 1) * 512], hf[:, c * 512:(c + 1) * 512],
                    SBUF_BIAS[:, BC["w2f"] + c:BC["w2f"] + c + 1], None, ALU.mult),
                    rd=["HF", "BB"], wr=["HF"])
            for c in range(NCH):
                P.step("tensor", lambda e, c=c: e.matmul(
                    PSA[0:64, :], CONSTS["C0"][:, c * 64:(c + 1) * 64],
                    hf[:, c * 512:(c + 1) * 512],
                    start=(c == 0), stop=(c == NCH - 1)), rd=["CONST", "HF"], wr=["pA"])
            P.step("vector", lambda e: e.tensor_scalar(gbf[0:64, :], PSA[0:64, :],
                                                       SBUF_BIAS[0:64, BC["b2"]:BC["b2"] + 1],
                                                       None, ALU.add),
                   rd=["pA", "BB"], wr=["GBF"])
            for j in range(QB // 128):
                ti = qb * (QB // 128) + j
                for half in range(2):
                    psq, pk = (PSW["0L"], "pW0L") if half == 0 else (PSW["0R"], "pW0R")
                    P.step("tensor", lambda e, j=j, half=half, psq=psq: e.matmul(
                        psq, gbf[0:64, j * 128:(j + 1) * 128],
                        wout[0:64, half * 512:(half + 1) * 512], start=True, stop=False),
                        rd=["GBF", "WOU"], wr=[pk])
                    P.step("tensor", lambda e, half=half, psq=psq: e.matmul(
                        psq, CONSTS["ONESROW"][0:1, :],
                        SROWS16[0:1, ROW_OFF["bout"] + half * 512:ROW_OFF["bout"] + (half + 1) * 512],
                        start=False, stop=True), rd=["ROW160", "CONST"], wr=[pk])
                P.dma("sync", XST[:, :], xprime[ti * 128:(ti + 1) * 128, :],
                      rd=["d_xp"], wr=["XST"])
                P.step("vector", lambda e: e.tensor_tensor(STG[:, 0:512], XST[:, 0:512],
                                                           PSW["0L"], ALU.add),
                       rd=["XST", "pW0L"], wr=["STGa"])
                P.step("vector", lambda e: e.tensor_tensor(STG[:, 512:1024], XST[:, 512:1024],
                                                           PSW["0R"], ALU.add),
                       rd=["XST", "pW0R"], wr=["STGb"])
                P.dma("sync", y_out[ti * 128:(ti + 1) * 128, :], STG[:, 0:1024],
                      rd=["STGa", "STGb"], wr=["d_y"])

    def copy_dram(dst, src, skey, dkey):
        P.barrier()
        for i in range(NT):
            P.dma("sync", XST[:, :], src[i * 128:(i + 1) * 128, :], rd=[skey], wr=["XST"])
            P.step("vector", lambda e: e.tensor_copy(STG[:, 0:1024], XST[:, :]),
                   rd=["XST"], wr=["STGa", "STGb"])
            P.dma("sync", dst[i * 128:(i + 1) * 128, :], STG[:, 0:1024],
                  rd=["STGa", "STGb"], wr=[dkey])

    def finish():
        with nc.Block() as blk:
            P.emit(nc, blk, sems)
        return nc

    for k in CONSTS:
        P.dma("sync", CONSTS[k][0:CONSTS[k].shape[0], :], CONST_IN[k][:, :], wr=["CONST"])

    bc_alloc("wqb", 8); bc_alloc("wkb", 8)
    bc_alloc("hb1", 32); bc_alloc("lb1", 4)
    bc_alloc("hgam", 8); bc_alloc("lgam", 8)
    bc_alloc("bq1", 8); bc_alloc("bk1", 8)
    bc_alloc("bin", 1); bc_alloc("b2", 1)
    bc_alloc("w1f", 8); bc_alloc("b1f", 8); bc_alloc("w2f", 8)
    load_vec_chunks(W["L0_attn_wq_b"], D, BC["wqb"])
    load_vec_chunks(W["L0_attn_wk_b"], D, BC["wkb"])
    load_vec_chunks(W["L0_ffn_heavy_b1"], DH, BC["hb1"])
    load_vec_chunks(W["L0_ffn_light_b1"], DL, BC["lb1"])
    load_vec_chunks(W["L0_ffn_heavy_gamma"], D, BC["hgam"])
    load_vec_chunks(W["L0_ffn_light_gamma"], D, BC["lgam"])
    load_vec_chunks(W["L1_attn_bq"], D, BC["bq1"])
    load_vec_chunks(W["L1_attn_bk"], D, BC["bk1"])
    load_vec_chunks(W["L1_ffn_W1"].rearrange("a b -> (a b)"), D, BC["w1f"])
    load_vec_chunks(W["L1_ffn_b1"].rearrange("a b -> (a b)"), D, BC["b1f"])
    load_vec_chunks(W["L1_ffn_W2"].rearrange("a b -> (a b)"), D, BC["w2f"])
    load_vec64_col(W["L1_ffn_bin_"], BC["bin"])
    load_vec64_col(W["L1_ffn_b2"], BC["b2"])

    phase_A(0, x_in, "d_x")
    if upto == "routing":
        phase_routing()
        P.step("vector", lambda e: e.tensor_copy(STG[:, 0:64], RBUF[:, 0:64]),
               rd=["RB"], wr=["STGa"])
        P.dma("sync", y_out[0:128, 0:64], STG[:, 0:64], rd=["STGa"], wr=["d_y"])
        P.dma("sync", y_out[128:256, 0:64], SMALL[:, 0:64], rd=["SM"], wr=["d_y"])
        P.dma("sync", y_out[256:384, 0:256], STG[:, 0:256], rd=["STGs", "STGt"], wr=["d_y"])
        return finish()
    load_row16(W["L0_attn_wv_b"], 0, "vb0")
    P.dma("gpsimd", AR(AR_WV, 8192).rearrange("p (c f) -> p c f", f=1024),
          W["L0_attn_wv_w"].rearrange("(c p) d -> c p d", p=128).transpose([1, 0, 2]),
          wr=["WV"])
    gen_KV(0, W["L0_attn_wk_w"].rearrange("(c p) d -> c p d", p=128), BC["wkb"], rope=True,
           vb_row=ROW_OFF["vb0"])
    phase_routing()
    load_row16(W["L0_attn_wo_b"], 0, "wob")
    P.dma("gpsimd", AR(AR_WV, 8192).rearrange("p (c f) -> p c f", f=1024),
          W["L0_attn_wo_w"].rearrange("(c p) d -> c p d", p=128).transpose([1, 0, 2]),
          rd=["WV"], wr=["WV"])
    for qb in range(NQB):
        attn_qblock(0, qb, W["L0_attn_wq_w"].rearrange("(c p) d -> c p d", p=128), BC["wqb"],
                    True, ROW_OFF["wob"], x_in, "d_x", xprime, "d_xp")
    if upto == "attn0":
        copy_dram(y_out, xprime, "d_xp", "d_y")
        return finish()
    load_row16(W["L0_ffn_light_b2"], 0, "lb2")
    load_row16(W["L0_ffn_heavy_b2"], 1, "hb2")
    transpose_l1_weights()
    ffn_l0()
    if upto == "ffn0":
        copy_dram(y_out, x_buf, "d_xb7", "d_y")
        return finish()

    phase_A(1, x_buf, "d_xb")
    load_row16(W["L1_attn_bv"], 0, "vb1")
    P.barrier()
    P.dma("gpsimd", AR(AR_WV, 8192).rearrange("p (c f) -> p c f", f=1024),
          wT_hbm["L1v"].transpose([1, 0, 2]), rd=["d_wT"], wr=["WV"])
    gen_KV(1, wT_hbm["L1k"], BC["bk1"], rope=False, vb_row=ROW_OFF["vb1"])
    P.dma("gpsimd", AR(AR_WV, 8192).rearrange("p (c f) -> p c f", f=1024),
          wT_hbm["L1o"].transpose([1, 0, 2]), rd=["WV", "d_wT"], wr=["WV"])
    for qb in range(NQB):
        attn_qblock(1, qb, wT_hbm["L1q"], BC["bq1"], False, None, x_buf, "d_xb",
                    xprime, "d_xp")
    if upto == "attn1":
        copy_dram(y_out, xprime, "d_xp", "d_y")
        return finish()
    load_row16(W["L1_ffn_bout"], 0, "bout")
    ffn_l1()
    return finish()


_prog_cache = {}


def _get_program():
    if "nc" not in _prog_cache:
        _prog_cache["nc"] = build_program()
    return _prog_cache["nc"]


def kernel(**inputs):
    nc = _get_program()
    consts = host_constants()
    x = np.ascontiguousarray(np.asarray(inputs["L0_attn_x"]), dtype=np.float32)
    base = {}
    for k, v in inputs.items():
        if k == "L0_attn_x":
            continue
        base[k] = np.ascontiguousarray(np.asarray(v), dtype=np.float32)
    for k, v in consts.items():
        base["c_" + k] = np.ascontiguousarray(v)
    in_maps = [dict(base, x=np.ascontiguousarray(x[i])) for i in range(8)]
    res = run_bass_kernel_spmd(nc, in_maps, core_ids=list(range(8)))
    return np.stack([np.asarray(r["y"]) for r in res.results]).astype(np.float32)


# version 08
# speedup vs baseline: 1.2178x; 1.2178x over previous
import numpy as np
import ml_dtypes
import concourse.bass as bass
import concourse.mybir as mybir
from concourse.bass_utils import run_bass_kernel_spmd

F32 = mybir.dt.float32
BF16 = mybir.dt.bfloat16
FP16 = mybir.dt.float16
AF = mybir.ActivationFunctionType
ALU = mybir.AluOpType
AX = mybir.AxisListType

N = 4096
D = 1024
NT = N // 128
NCH = D // 128
QB = 512
NQB = N // QB
FB = 256
NFB = N // FB
DH, DL = 4096, 512
NUM_HEAVY = 1024
COOR_ITERS = 50
BISECT_ITERS = 40
EPS_LN = 1e-5

ENGINES = ["tensor", "vector", "scalar", "gpsimd", "sync"]
SELF_SAFE = {"tensor"}
RING_DEPTH = {"sync": 10, "gpsimd": 6}


class Pipe:
    def __init__(self):
        self.steps = []
        self.sem_names = (["c_tensor", "c_vector", "c_scalar", "c_gpsimd"]
                          + [f"q_sync{i}" for i in range(RING_DEPTH["sync"])]
                          + [f"q_gpsimd{i}" for i in range(RING_DEPTH["gpsimd"])])
        self.count = {s: 0 for s in self.sem_names}
        self.qpos = {"sync": 0, "gpsimd": 0}
        self.know = {e: {s: 0 for s in self.sem_names} for e in ENGINES}
        self.clk = {s: [] for s in self.sem_names}
        self.last_write = {}
        self.readers = {}
        self.nwaits = 0

    def _deps(self, reads, writes):
        m = {}
        for k in reads:
            w = self.last_write.get(k)
            if w and w[1] > m.get(w[0], (0, None))[0]:
                m[w[0]] = (w[1], w[2])
        for k in writes:
            w = self.last_write.get(k)
            if w and w[1] > m.get(w[0], (0, None))[0]:
                m[w[0]] = (w[1], w[2])
            for r in self.readers.get(k, ()):
                if r[1] > m.get(r[0], (0, None))[0]:
                    m[r[0]] = (r[1], r[2])
        return m

    def _emit(self, engine, fn, reads, writes, sem, amt, extra_dep=None):
        know = self.know[engine]
        waits = []
        m = self._deps(reads + ["BAR"], writes)
        if extra_dep:
            s, v = extra_dep
            if v > 0 and v > m.get(s, (0, None))[0]:
                m[s] = (v, None)
        own = "c_" + engine
        for s in sorted(m):
            v, c = m[s]
            if s == own and engine in SELF_SAFE:
                continue
            if know[s] >= v:
                continue
            waits.append((s, v))
            self.nwaits += 1
            know[s] = v
            if c is not None:
                for s2, v2 in c.items():
                    if v2 > know[s2]:
                        know[s2] = v2
        self.count[sem] += amt
        val = self.count[sem]
        cl = dict(know)
        cl[sem] = val
        self.clk[sem].append(cl)
        self.steps.append((engine, fn, waits, (sem, amt)))
        rec = (sem, val, cl)
        for k in writes:
            self.last_write[k] = rec
            self.readers[k] = []
        for k in reads:
            self.readers.setdefault(k, []).append(rec)

    def step(self, engine, fn, rd=(), wr=()):
        self._emit(engine, fn, list(rd), list(wr), "c_" + engine, 1)

    def dma(self, queue, out, in_, rd=(), wr=(), **kw):
        i = self.qpos[queue]
        self.qpos[queue] += 1
        depth = RING_DEPTH[queue]
        sem = f"q_{queue}{i % depth}"
        extra = (sem, 16 * (i // depth)) if i >= depth else None
        fn = (lambda o, s, kw2: (lambda e: e.dma_start(o, s, **kw2)))(out, in_, kw)
        self._emit(queue, fn, list(rd), list(wr), sem, 16, extra_dep=extra)

    def barrier(self):
        keys = (set(self.last_write) | set(self.readers)) - {"BAR"}
        self._emit("gpsimd", lambda e: None, [], list(keys) + ["BAR"], "c_gpsimd", 1)

    def fence(self, keys):
        self._emit("gpsimd", lambda e: None, [], list(keys) + ["BAR"], "c_gpsimd", 1)

    def emit(self, nc, blk, sems):
        finals = dict(self.count)
        for name in ENGINES:
            def make(nm):
                def emit_eng(eng):
                    for en, fn, waits, (sem, amt) in self.steps:
                        if en != nm:
                            continue
                        for s, v in waits:
                            eng.wait_ge(sems[s], v)
                        inst = fn(eng)
                        if inst is None:
                            inst = eng.nop()
                        inst.then_inc(sems[sem], amt)
                    if nm == "sync":
                        for s, v in finals.items():
                            if v > 0:
                                eng.wait_ge(sems[s], v)
                return emit_eng
            getattr(blk, name)(make(name))


def host_constants():
    c = {}
    bf = ml_dtypes.bfloat16
    c["IDENT"] = np.eye(128, dtype=bf)
    c["IDENT32"] = np.eye(128, dtype=np.float32)
    c["ONESROW"] = np.ones((1, 128), dtype=bf)
    c["ONESCOL"] = np.ones((128, 1), dtype=np.float32)
    c["LTRI"] = np.triu(np.ones((128, 128), dtype=np.float32))
    fraction = (2 * np.arange(0, 32) / 64).astype(np.float32)
    timescale = (np.float32(10000.0) ** fraction).astype(np.float32)
    pos = np.arange(N).astype(np.float32)
    sinu = (pos[None, :] / timescale[:, None]).astype(np.float32)
    sin = np.sin(sinu).astype(np.float32)
    cos = np.cos(sinu).astype(np.float32)
    c["COS128"] = np.concatenate([cos, cos, cos, cos], 0).astype(bf)
    c["SINSIG"] = np.concatenate([-sin, sin, -sin, sin], 0).astype(bf)
    r0 = np.zeros((64, 1024), dtype=bf)
    for m in range(64):
        r0[m, m * 16:(m + 1) * 16] = 1
    c["R0"] = r0
    c0 = np.zeros((128, 8 * 64), dtype=bf)
    for k in range(8):
        for mm in range(8):
            m = 8 * k + mm
            c0[mm * 16:(mm + 1) * 16, k * 64 + m] = 1
    c["C0"] = c0
    return c


AR_K = 0
AR_V = 32768
AR_WV = 66048
AR_QSL = 74240
AR_XSL = 78336
AR_CTX = 82432
AR_PB = 86528
AR_RT0 = 87552
AR_RT1 = 88064
AR_CTXT = 88576
AR_RSL = 89088
AR_END = 90112
AR_W1H = 0
AR_W2H = 32768
AR_HB = 65536
AR_LHB = 73728
AR_XRB = 74752
AR_LW1 = 76800
AR_LW2 = 80896
AR_WIN = 0
AR_SBF = 512
AR_HF = 1024
AR_GBF = 5120
AR_WOUT = 5632


def build_program(upto="all"):
    nc = bass.Bass("TRN2", target_bir_lowering=False)
    P = Pipe()

    def din(name, shape, dtype=F32):
        return nc.dram_tensor(name, list(shape), dtype, kind="ExternalInput").ap()

    x_in = din("x", (N, D))
    WSHAPES = [
        ("L0_attn_wq_w", (D, D)), ("L0_attn_wq_b", (D,)),
        ("L0_attn_wk_w", (D, D)), ("L0_attn_wk_b", (D,)),
        ("L0_attn_wv_w", (D, D)), ("L0_attn_wv_b", (D,)),
        ("L0_attn_wo_w", (D, D)), ("L0_attn_wo_b", (D,)),
        ("L0_ffn_routing_token", (D,)),
        ("L0_ffn_light_gamma", (D,)), ("L0_ffn_light_w1", (D, DL)),
        ("L0_ffn_light_b1", (DL,)), ("L0_ffn_light_w2", (DL, D)),
        ("L0_ffn_light_b2", (D,)),
        ("L0_ffn_heavy_gamma", (D,)), ("L0_ffn_heavy_w1", (D, DH)),
        ("L0_ffn_heavy_b1", (DH,)), ("L0_ffn_heavy_w2", (DH, D)),
        ("L0_ffn_heavy_b2", (D,)),
        ("L1_attn_Wq", (D, D)), ("L1_attn_bq", (D,)),
        ("L1_attn_Wk", (D, D)), ("L1_attn_bk", (D,)),
        ("L1_attn_Wv", (D, D)), ("L1_attn_bv", (D,)),
        ("L1_attn_Wo", (D, D)),
        ("L1_ffn_Win", (D, 64)), ("L1_ffn_bin_", (64,)),
        ("L1_ffn_W1", (64, 16)), ("L1_ffn_b1", (64, 16)),
        ("L1_ffn_W2", (64, 16)), ("L1_ffn_b2", (64,)),
        ("L1_ffn_Wout", (64, D)), ("L1_ffn_bout", (D,)),
    ]
    W = {nm: din(nm, shp) for nm, shp in WSHAPES}
    CSPEC = host_constants()
    CONST_IN = {k: din("c_" + k, v.shape,
                       BF16 if v.dtype == ml_dtypes.bfloat16 else F32)
                for k, v in CSPEC.items()}

    y_out = nc.dram_tensor("y", [N, D], F32, kind="ExternalOutput").ap()
    x_buf = nc.dram_tensor("x_buf", [N, D], F32, kind="Internal").ap()
    xprime = nc.dram_tensor("xprime", [N, D], F32, kind="Internal").ap()
    xnT_hbm = nc.dram_tensor("xnT_hbm", [NCH, 128, N], BF16, kind="Internal").ap()
    xrT_hbm = nc.dram_tensor("xrT_hbm", [NCH, 128, N], BF16, kind="Internal").ap()
    wT_hbm = {nm: nc.dram_tensor(nm + "_T", [NCH, 128, D], BF16, kind="Internal").ap()
              for nm in ["L1q", "L1k", "L1v", "L1o"]}

    sb = nc.alloc_sbuf_tensor
    ARENA = sb("ARENA", [128, AR_END], BF16)

    def AR(off, nels):
        return ARENA[:, off:off + nels]

    WST = sb("WST", [128, 2048], BF16)
    STGX = sb("STGX", [128, 3072], F32)
    STG = STGX[:, 0:2048]
    XST = STGX[:, 2048:3072]
    REPB = sb("REPB", [128, 1024], F32)
    SMALL = sb("SMALLF", [128, 64], F32)
    SBUF_BIAS = sb("SBUF_BIAS", [128, 128], F32)
    SROWS16 = sb("SROWS16", [1, 2048], BF16)
    RBUF = sb("RBUF", [128, 64], F32)
    LNB = sb("LNB", [128, 32], F32)
    CONSTS = {
        "IDENT": sb("cIDENT", [128, 128], BF16),
        "IDENT32": sb("cIDENT32", [128, 128], F32),
        "ONESROW": sb("cONESROW", [1, 128], BF16),
        "ONESCOL": sb("cONESCOL", [128, 1], F32),
        "LTRI": sb("cLTRI", [128, 128], F32),
        "R0": sb("cR0", [64, 1024], BF16),
        "C0": sb("cC0", [128, 8 * 64], BF16),
    }
    LT_ROW0 = CONSTS["LTRI"]

    ps = nc.alloc_psum_tensor
    PSW0 = ps("PSW0", [128, 1024], F32)
    PSW1 = ps("PSW1", [128, 1024], F32)
    PSA = ps("PSA", [128, 512], F32)
    PSB = ps("PSB", [128, 512], F32)
    PSC = ps("PSC", [128, 512], F32)
    PSD = ps("PSD", [128, 512], F32)
    PSW = {"0L": PSW0[:, 0:512], "0R": PSW0[:, 512:1024],
           "1L": PSW1[:, 0:512], "1R": PSW1[:, 512:1024]}

    sems = {s: nc.alloc_semaphore(s) for s in P.sem_names}

    BC = {}
    _bc = [0]

    def bc_alloc(name, k):
        BC[name] = _bc[0]
        _bc[0] += k

    ROW_OFF = {}

    def load_row16(vec_ap, slot, name):
        ROW_OFF[name] = slot * 1024
        P.dma("gpsimd", SROWS16[0:1, slot * 1024:(slot + 1) * 1024], vec_ap[None, :],
              wr=[f"ROW16{slot}"])

    def load_vec_chunks(vec_ap, n, dst_col0):
        k = n // 128
        P.dma("sync", XST[0:k, 0:128], vec_ap.rearrange("(k p) -> k p", p=128), wr=["XST"])
        P.step("tensor", lambda e, k=k: e.matmul(
            PSC[:, 0:k], XST[0:k, 0:128], CONSTS["IDENT32"][0:k, 0:k],
            start=True, stop=True), rd=["XST", "CONST"], wr=["pC"])
        P.step("vector", lambda e, k=k, dst_col0=dst_col0: e.tensor_copy(
            SBUF_BIAS[:, dst_col0:dst_col0 + k], PSC[:, 0:k]), rd=["pC"], wr=["BB"])

    def load_vec64_col(vec_ap, dst_col):
        P.dma("sync", XST[0:1, 0:64], vec_ap[None, :], wr=["XST"])
        P.step("tensor", lambda e: e.matmul(PSC[0:64, 0:1], XST[0:1, 0:64],
                                            CONSTS["ONESCOL"][0:1, 0:1], start=True, stop=True),
               rd=["XST", "CONST"], wr=["pC"])
        P.step("vector", lambda e, dst_col=dst_col: e.tensor_copy(
            SBUF_BIAS[0:64, dst_col:dst_col + 1], PSC[0:64, 0:1]), rd=["pC"], wr=["BB"])

    def rep_row_f32(vec_ap, dst_sb, dst_key):
        P.dma("sync", XST[0:1, 0:1024], vec_ap[None, :], wr=["XST"])
        for t in range(2):
            psq, pk = (PSC, "pC") if t == 0 else (PSD, "pD")
            P.step("tensor", lambda e, t=t, psq=psq: e.matmul(
                psq[:, :], LT_ROW0[0:1, :], XST[0:1, t * 512:(t + 1) * 512],
                start=True, stop=True), rd=["XST", "CONST"], wr=[pk])
        for t in range(2):
            psq, pk = (PSC, "pC") if t == 0 else (PSD, "pD")
            P.step("vector", lambda e, t=t, psq=psq: e.tensor_copy(
                dst_sb[:, t * 512:(t + 1) * 512], psq[:, :]), rd=[pk], wr=[dst_key])

    def stream_w_tiles(w3d, fchunk, slot, extra_rd=()):
        P.dma("gpsimd", WST[:, slot * 1024:(slot + 1) * 1024].rearrange("p (c f) -> p c f", f=128),
              w3d[:, :, fchunk * 128:(fchunk + 1) * 128].transpose([1, 0, 2]),
              rd=list(extra_rd), wr=[f"WST{slot}"])

    def cross_sum_11(src_col, dst_col):
        P.step("tensor", lambda e: e.matmul(PSA[0:1, 0:1], SMALL[:, src_col:src_col + 1],
                                            CONSTS["ONESCOL"][:, 0:1], start=True, stop=True),
               rd=["SM", "CONST"], wr=["pA"])
        P.step("vector", lambda e: e.tensor_copy(SMALL[0:1, dst_col:dst_col + 1], PSA[0:1, 0:1]),
               rd=["pA"], wr=["SM"])

    def rep_11(src_col, dst_col):
        P.step("tensor", lambda e: e.matmul(PSA[:, 0:1], LT_ROW0[0:1, :],
                                            SMALL[0:1, src_col:src_col + 1], start=True, stop=True),
               rd=["SM", "CONST"], wr=["pA"])
        P.step("vector", lambda e: e.tensor_copy(SMALL[:, dst_col:dst_col + 1], PSA[:, 0:1]),
               rd=["pA"], wr=["SM"])

    def swap_dmas(rt0, rt1, k0, k1):
        P.dma("sync", rt1[0:32, :], rt0[32:64, :], rd=[k0], wr=[k1])
        P.dma("sync", rt1[32:64, :], rt0[0:32, :], rd=[k0], wr=[k1])
        P.dma("sync", rt1[64:96, :], rt0[96:128, :], rd=[k0], wr=[k1])
        P.dma("sync", rt1[96:128, :], rt0[64:96, :], rd=[k0], wr=[k1])

    def phase_A(layer, src_x, xkey):
        if layer == 0:
            P.barrier()
        else:
            P.fence(["CTXe", "CTXa1", "CTXb1", "PBa", "PBb", "RT0", "RT1", "CTT",
                     "XST", "STGcd", "STGc", "STGd", "STGa", "STGb", "SM0", "SM1",
                     "SMe", "SMg", "LNB0", "LNB1", "STA0", "STA1", "STB0", "STB1", "RSL"])
        P.step("vector", lambda e: e.memset(SMALL[:, 40:41], float(EPS_LN)), wr=["SMe"])
        if layer == 0:
            rep_row_f32(W["L0_ffn_routing_token"], REPB, "REPB")
            P.step("vector", lambda e: e.reduce_sum(SMALL[:, 8:9], REPB[:, :], axis=AX.X),
                   rd=["REPB"], wr=["SMg"])
        for i in range(NT):
            par = i % 2
            o = 16 * par
            smk = f"SM{par}"
            lnk = f"LNB{par}"
            lo = 16 * par
            xt = XST[:, :] if par == 0 else STG[:, 1024:2048]
            xtk = "XST" if par == 0 else "STGcd"
            cak, cbk = f"CTXa{par}", f"CTXb{par}"
            if layer == 0:
                ctxo = AR_CTX if par == 0 else AR_CTX + 2048
                xn_tm = AR(ctxo, 1024)
                xr_tm = AR(ctxo + 1024, 1024)
                sta = AR(AR_QSL + 1024 * par, 1024)
                stb = AR(AR_QSL + 2048 + 1024 * par, 1024)
            else:
                xn_tm = AR(84992 + 1024 * par, 1024)
                xr_tm = None
                sta = AR(87040 + 1024 * par, 1024)
                stb = None
            stak = f"STA{par}"
            stbk = f"STB{par}"
            P.dma("sync", xt, src_x[i * 128:(i + 1) * 128, :],
                  rd=[xkey if layer == 0 else f"d_xb{i // 4}"], wr=[xtk])
            P.step("vector", lambda e, xt=xt, lo=lo: e.bn_stats(LNB[:, lo:lo + 6], xt[:, 0:512]),
                   rd=[xtk], wr=[lnk])
            P.step("vector", lambda e, xt=xt, lo=lo: e.bn_stats(LNB[:, lo + 6:lo + 12], xt[:, 512:1024]),
                   rd=[xtk], wr=[lnk])
            P.step("vector", lambda e, o=o, lo=lo: e.bn_aggr(SMALL[:, o:o + 2], LNB[:, lo:lo + 12]),
                   rd=[lnk], wr=[smk])
            P.step("scalar", lambda e, o=o: e.activation(SMALL[:, o + 2:o + 3], SMALL[:, o + 1:o + 2],
                                                         AF.Ln, bias=SMALL[:, 40:41]),
                   rd=[smk, "SMe"], wr=[smk])
            P.step("scalar", lambda e, o=o: e.activation(SMALL[:, o + 2:o + 3], SMALL[:, o + 2:o + 3],
                                                         AF.Exp, scale=-0.5), rd=[smk], wr=[smk])
            P.step("vector", lambda e, o=o: e.tensor_tensor(SMALL[:, o + 3:o + 4], SMALL[:, o:o + 1],
                                                            SMALL[:, o + 2:o + 3], ALU.mult),
                   rd=[smk], wr=[smk])
            P.step("vector", lambda e, o=o: e.tensor_scalar(SMALL[:, o + 3:o + 4], SMALL[:, o + 3:o + 4],
                                                            -1.0, None, ALU.mult), rd=[smk], wr=[smk])
            P.step("scalar", lambda e, o=o, xn_tm=xn_tm, xt=xt: e.activation(
                xn_tm, xt, AF.Identity, bias=SMALL[:, o + 3:o + 4], scale=SMALL[:, o + 2:o + 3]),
                rd=[xtk, smk], wr=[cak])
            if layer == 0:
                P.step("vector", lambda e, o=o: e.tensor_tensor(SMALL[:, o + 5:o + 6], SMALL[:, o + 2:o + 3], SMALL[:, o + 2:o + 3], ALU.mult), rd=[smk], wr=[smk])
                P.step("vector", lambda e, o=o: e.tensor_tensor(SMALL[:, o + 5:o + 6], SMALL[:, o + 5:o + 6], SMALL[:, o + 1:o + 2], ALU.mult), rd=[smk], wr=[smk])
                P.step("vector", lambda e, o=o: e.tensor_scalar(SMALL[:, o + 5:o + 6], SMALL[:, o + 5:o + 6], 1024.0, None, ALU.mult), rd=[smk], wr=[smk])
                P.step("scalar", lambda e, o=o: e.activation(SMALL[:, o + 5:o + 6], SMALL[:, o + 5:o + 6], AF.Ln), rd=[smk], wr=[smk])
                P.step("scalar", lambda e, o=o: e.activation(SMALL[:, o + 5:o + 6], SMALL[:, o + 5:o + 6], AF.Exp, scale=0.5), rd=[smk], wr=[smk])
                P.step("vector", lambda e, o=o: e.tensor_scalar(SMALL[:, o + 5:o + 6], SMALL[:, o + 5:o + 6], 1e-12, None, ALU.max), rd=[smk], wr=[smk])
                P.step("vector", lambda e, o=o: e.reciprocal(SMALL[:, o + 4:o + 5], SMALL[:, o + 5:o + 6]), rd=[smk], wr=[smk])
                P.step("vector", lambda e, o=o: e.tensor_scalar(SMALL[:, o + 4:o + 5], SMALL[:, o + 4:o + 5], 32.0, None, ALU.mult), rd=[smk], wr=[smk])
                P.step("vector", lambda e, o=o, xt=xt: e.scalar_tensor_tensor(
                    STG[:, 0:1024], xt, 1.0, REPB[:, :], ALU.mult, ALU.mult,
                    accum_out=SMALL[:, o + 6:o + 7]), rd=[xtk, "REPB"], wr=["STGa", "STGb", smk])
                P.step("vector", lambda e, o=o: e.tensor_tensor(SMALL[:, o + 7:o + 8], SMALL[:, o:o + 1], SMALL[:, 8:9], ALU.mult), rd=[smk, "SMg"], wr=[smk])
                P.step("vector", lambda e, o=o: e.tensor_tensor(SMALL[:, o + 7:o + 8], SMALL[:, o + 6:o + 7], SMALL[:, o + 7:o + 8], ALU.subtract), rd=[smk], wr=[smk])
                P.step("vector", lambda e, i=i, o=o: e.tensor_tensor(RBUF[:, i:i + 1], SMALL[:, o + 7:o + 8], SMALL[:, o + 2:o + 3], ALU.mult), rd=[smk], wr=["RB"])
                P.step("vector", lambda e, o=o, xr_tm=xr_tm, xn_tm=xn_tm: e.tensor_scalar(
                    xr_tm, xn_tm, SMALL[:, o + 4:o + 5], None, ALU.mult), rd=[cak, smk], wr=[cbk])
            for half in range(2):
                psq, pk = (PSC, "pC") if half == 0 else (PSD, "pD")
                for j in range(4):
                    jj = half * 4 + j
                    P.step("tensor", lambda e, jj=jj, psq=psq, j=j, xn_tm=xn_tm: e.matmul(
                        psq[:, j * 128:(j + 1) * 128], xn_tm[:, jj * 128:(jj + 1) * 128],
                        CONSTS["IDENT"][:, :], start=True, stop=True),
                        rd=[cak, "CONST"], wr=[pk])
                P.step("vector", lambda e, half=half, psq=psq, sta=sta: e.tensor_copy(
                    sta[:, half * 512:(half + 1) * 512], psq[:, 0:512]), rd=[pk], wr=[stak])
            P.dma("sync", xnT_hbm[:, :, i * 128:(i + 1) * 128].transpose([1, 0, 2]),
                  sta.rearrange("p (c n) -> p c n", n=128), rd=[stak], wr=[f"d_xnT{i // 4}"])
            if layer == 0:
                for half in range(2):
                    psq, pk = (PSC, "pC") if half == 0 else (PSD, "pD")
                    for j in range(4):
                        jj = half * 4 + j
                        P.step("tensor", lambda e, jj=jj, psq=psq, j=j, xr_tm=xr_tm: e.matmul(
                            psq[:, j * 128:(j + 1) * 128], xr_tm[:, jj * 128:(jj + 1) * 128],
                            CONSTS["IDENT"][:, :], start=True, stop=True),
                            rd=[cbk, "CONST"], wr=[pk])
                    P.step("vector", lambda e, half=half, psq=psq, stb=stb: e.tensor_copy(
                        stb[:, half * 512:(half + 1) * 512], psq[:, 0:512]), rd=[pk], wr=[stbk])
                P.dma("sync", xrT_hbm[:, :, i * 128:(i + 1) * 128].transpose([1, 0, 2]),
                      stb.rearrange("p (c n) -> p c n", n=128), rd=[stbk], wr=[f"d_xrT{i // 2}"])

    def phase_routing():
        sS = RBUF[:, 0:32]
        Kf = float(np.float32(min(NUM_HEAVY * 9.0 / 8.0, N)))
        logk = float(np.log(np.float32(Kf)).astype(np.float32))
        V = lambda fn, rd=(), wr=(): P.step("vector", fn, rd=rd, wr=wr)
        SC = lambda fn, rd=(), wr=(): P.step("scalar", fn, rd=rd, wr=wr)
        sB = RBUF[:, 32:64]
        V(lambda e: e.tensor_scalar(sB, sS, -1.0, None, ALU.mult), rd=["RB"], wr=["RBb"])
        for it in range(COOR_ITERS):
            if it == 0:
                V(lambda e: e.tensor_tensor(STG[:, 0:32], sS, sB, ALU.add),
                  rd=["RB", "RBb"], wr=["STGa"])
            else:
                V(lambda e: e.tensor_scalar(STG[:, 32:64], sS, SMALL[:, 13:14], None, ALU.add),
                  rd=["RB", "SM"], wr=["STGa"])
                V(lambda e: e.tensor_scalar(sB, STG[:, 32:64], 0.0, -1.0, ALU.max, ALU.mult),
                  rd=["STGa"], wr=["RBb"])
                V(lambda e: e.tensor_tensor(STG[:, 0:32], sS, sB, ALU.add),
                  rd=["RB", "RBb"], wr=["STGa"])
            V(lambda e: e.reduce_max(SMALL[:, 16:17], STG[:, 0:32], axis=AX.X),
              rd=["STGa"], wr=["SM"])
            P.step("tensor", lambda e: e.matmul(PSA[0:1, 0:128], SMALL[:, 16:17],
                                                CONSTS["IDENT32"][:, :], start=True, stop=True),
                   rd=["SM", "CONST"], wr=["pA"])
            V(lambda e: e.tensor_copy(STG[0:1, 128:256], PSA[0:1, 0:128]), rd=["pA"], wr=["STGu"])
            V(lambda e: e.reduce_max(SMALL[0:1, 17:18], STG[0:1, 128:256], axis=AX.X),
              rd=["STGu"], wr=["SM"])
            rep_11(17, 18)
            V(lambda e: e.tensor_scalar(SMALL[:, 19:20], SMALL[:, 18:19], -1.0, None, ALU.mult),
              rd=["SM"], wr=["SM"])
            SC(lambda e: e.activation(STG[:, 32:64], STG[:, 0:32], AF.Exp,
                                      bias=SMALL[:, 19:20], accum_out=SMALL[:, 10:11]),
               rd=["STGa", "SM"], wr=["STGt", "SM"])
            cross_sum_11(10, 11)
            SC(lambda e: e.activation(SMALL[0:1, 14:15], SMALL[0:1, 11:12], AF.Ln),
               rd=["SM"], wr=["SM"])
            V(lambda e: e.tensor_tensor(SMALL[0:1, 14:15], SMALL[0:1, 14:15],
                                        SMALL[0:1, 17:18], ALU.add), rd=["SM"], wr=["SM"])
            V(lambda e: e.tensor_scalar(SMALL[0:1, 12:13], SMALL[0:1, 14:15],
                                        logk, -1.0, ALU.subtract, ALU.mult), rd=["SM"], wr=["SM"])
            rep_11(12, 13)
        V(lambda e: e.tensor_scalar(STG[:, 0:32], sS, SMALL[:, 13:14], None, ALU.add),
          rd=["RB", "SM"], wr=["STGa"])
        V(lambda e: e.tensor_scalar(STG[:, 32:64], STG[:, 0:32], 0.0, -1.0, ALU.max, ALU.mult),
          rd=["STGa"], wr=["STGa"])
        V(lambda e: e.tensor_tensor(STG[:, 0:32], STG[:, 0:32], STG[:, 32:64], ALU.add),
          rd=["STGa"], wr=["STGa"])
        SC(lambda e: e.activation(STG[:, 64:96], STG[:, 0:32], AF.Exp), rd=["STGa"], wr=["STGs"])
        scores = STG[:, 64:96]
        V(lambda e: e.memset(SMALL[0:1, 20:21], 0.0), rd=["SM"], wr=["SM"])
        V(lambda e: e.memset(SMALL[0:1, 21:22], 1.5), rd=["SM"], wr=["SM"])
        for it in range(BISECT_ITERS):
            V(lambda e: e.tensor_tensor(SMALL[0:1, 22:23], SMALL[0:1, 20:21],
                                        SMALL[0:1, 21:22], ALU.add), rd=["SM"], wr=["SM"])
            V(lambda e: e.tensor_scalar(SMALL[0:1, 22:23], SMALL[0:1, 22:23],
                                        0.5, None, ALU.mult), rd=["SM"], wr=["SM"])
            rep_11(22, 23)
            V(lambda e: e.tensor_scalar(STG[:, 96:128], scores, SMALL[:, 23:24],
                                        0.0, ALU.is_ge, ALU.add, accum_out=SMALL[:, 10:11]),
              rd=["STGs", "SM"], wr=["STGt", "SM"])
            cross_sum_11(10, 24)
            V(lambda e: e.tensor_scalar(SMALL[0:1, 25:26], SMALL[0:1, 24:25],
                                        float(NUM_HEAVY), None, ALU.is_ge), rd=["SM"], wr=["SM"])
            V(lambda e: e.tensor_tensor(SMALL[0:1, 26:27], SMALL[0:1, 22:23],
                                        SMALL[0:1, 20:21], ALU.subtract), rd=["SM"], wr=["SM"])
            V(lambda e: e.tensor_tensor(SMALL[0:1, 26:27], SMALL[0:1, 26:27],
                                        SMALL[0:1, 25:26], ALU.mult), rd=["SM"], wr=["SM"])
            V(lambda e: e.tensor_tensor(SMALL[0:1, 20:21], SMALL[0:1, 20:21],
                                        SMALL[0:1, 26:27], ALU.add), rd=["SM"], wr=["SM"])
            V(lambda e: e.tensor_scalar(SMALL[0:1, 27:28], SMALL[0:1, 25:26],
                                        1.0, -1.0, ALU.subtract, ALU.mult), rd=["SM"], wr=["SM"])
            V(lambda e: e.tensor_tensor(SMALL[0:1, 26:27], SMALL[0:1, 22:23],
                                        SMALL[0:1, 21:22], ALU.subtract), rd=["SM"], wr=["SM"])
            V(lambda e: e.tensor_tensor(SMALL[0:1, 26:27], SMALL[0:1, 26:27],
                                        SMALL[0:1, 27:28], ALU.mult), rd=["SM"], wr=["SM"])
            V(lambda e: e.tensor_tensor(SMALL[0:1, 21:22], SMALL[0:1, 21:22],
                                        SMALL[0:1, 26:27], ALU.add), rd=["SM"], wr=["SM"])
        rep_11(21, 23)
        V(lambda e: e.tensor_scalar(STG[:, 96:128], scores, SMALL[:, 23:24],
                                    None, ALU.is_lt), rd=["STGs", "SM"], wr=["STGt"])
        V(lambda e: e.tensor_tensor(STG[:, 96:128], STG[:, 96:128], scores, ALU.mult),
          rd=["STGt", "STGs"], wr=["STGt"])
        V(lambda e: e.reduce_max(SMALL[:, 10:11], STG[:, 96:128], axis=AX.X), rd=["STGt"], wr=["SM"])
        P.step("tensor", lambda e: e.matmul(PSA[0:1, 0:128], SMALL[:, 10:11],
                                            CONSTS["IDENT32"][:, :], start=True, stop=True),
               rd=["SM", "CONST"], wr=["pA"])
        V(lambda e: e.tensor_copy(STG[0:1, 128:256], PSA[0:1, 0:128]), rd=["pA"], wr=["STGu"])
        V(lambda e: e.reduce_max(SMALL[0:1, 24:25], STG[0:1, 128:256], axis=AX.X),
          rd=["STGu"], wr=["SM"])
        rep_11(24, 25)
        V(lambda e: e.tensor_scalar(STG[:, 96:128], scores, SMALL[:, 25:26],
                                    0.0, ALU.is_gt, ALU.add, accum_out=SMALL[:, 10:11]),
          rd=["STGs", "SM"], wr=["STGt", "SM"])
        V(lambda e: e.tensor_scalar(STG[:, 128:160], scores, SMALL[:, 25:26],
                                    None, ALU.is_equal), rd=["STGs", "SM"], wr=["STGu"])
        cross_sum_11(10, 26)
        V(lambda e: e.tensor_scalar(SMALL[0:1, 27:28], SMALL[0:1, 26:27],
                                    float(NUM_HEAVY), -1.0, ALU.subtract, ALU.mult),
          rd=["SM"], wr=["SM"])
        rep_11(27, 28)
        P.step("tensor", lambda e: e.matmul(PSA[:, 0:32], CONSTS["LTRI"][:, :],
                                            STG[:, 128:160], start=True, stop=True),
               rd=["STGu", "CONST"], wr=["pA"])
        V(lambda e: e.tensor_copy(STG[:, 160:192], PSA[:, 0:32]), rd=["pA"], wr=["STGv"])
        P.step("tensor", lambda e: e.matmul(PSB[0:32, 0:1], STG[:, 128:160],
                                            CONSTS["ONESCOL"][:, 0:1], start=True, stop=True),
               rd=["STGu", "CONST"], wr=["pB"])
        V(lambda e: e.tensor_copy(SMALL[0:32, 30:31], PSB[0:32, 0:1]), rd=["pB"], wr=["SM"])
        P.step("tensor", lambda e: e.matmul(PSB[0:32, 1:2], CONSTS["LTRI"][0:32, 0:32],
                                            SMALL[0:32, 30:31], start=True, stop=True),
               rd=["SM", "CONST"], wr=["pB"])
        V(lambda e: e.tensor_copy(SMALL[0:32, 31:32], PSB[0:32, 1:2]), rd=["pB"], wr=["SM"])
        V(lambda e: e.tensor_tensor(SMALL[0:32, 31:32], SMALL[0:32, 31:32],
                                    SMALL[0:32, 30:31], ALU.subtract), rd=["SM"], wr=["SM"])
        P.step("tensor", lambda e: e.matmul(PSB[0:1, 0:32], SMALL[0:32, 31:32],
                                            CONSTS["IDENT32"][0:32, 0:32], start=True, stop=True),
               rd=["SM", "CONST"], wr=["pB"])
        V(lambda e: e.tensor_copy(STG[0:1, 192:224], PSB[0:1, 0:32]), rd=["pB"], wr=["STGw"])
        P.step("tensor", lambda e: e.matmul(PSB[:, 0:32], LT_ROW0[0:1, :],
                                            STG[0:1, 192:224], start=True, stop=True),
               rd=["STGw", "CONST"], wr=["pB"])
        V(lambda e: e.tensor_copy(STG[:, 192:224], PSB[:, 0:32]), rd=["pB"], wr=["STGw"])
        V(lambda e: e.tensor_tensor(STG[:, 160:192], STG[:, 160:192],
                                    STG[:, 192:224], ALU.add), rd=["STGv", "STGw"], wr=["STGv"])
        V(lambda e: e.tensor_scalar(STG[:, 160:192], STG[:, 160:192],
                                    SMALL[:, 28:29], None, ALU.is_le), rd=["STGv", "SM"], wr=["STGv"])
        V(lambda e: e.tensor_tensor(STG[:, 160:192], STG[:, 160:192],
                                    STG[:, 128:160], ALU.mult), rd=["STGv", "STGu"], wr=["STGv"])
        V(lambda e: e.tensor_tensor(RBUF[:, 32:64], STG[:, 96:128],
                                    STG[:, 160:192], ALU.add), rd=["STGt", "STGv"], wr=["RB"])

    def gen_KV(layer, wk3d, bk_col, rope, vb_row=0):
        if layer == 1:
            P.barrier()
        P.step("gpsimd", lambda e: e.memset(
            AR(AR_V, 33280).bitcast(FP16).rearrange("p (g x) -> p g x", x=65)[:, :, 64:65], 1.0),
            wr=["V"])
        xsl = AR(AR_XSL, 4096).rearrange("p (c n) -> p c n", n=QB)
        Kb = AR(AR_K, 32768).rearrange("p (c n) -> p c n", n=N)
        Vv = AR(AR_V, 33280).bitcast(FP16).rearrange("p (t g) -> p t g", g=1040)
        rsl_c = AR(AR_RSL, 512)
        rsl_s = AR(AR_RSL + 512, 512)
        rpair = [(AR(AR_RT0, 512), AR(AR_RT1, 512), "RT0", "RT1"),
                 (AR(AR_PB, 512), AR(AR_PB + 512, 512), "PBa", "PBb")]
        for nb in range(NQB):
            P.dma("sync", xsl, xnT_hbm[:, :, nb * QB:(nb + 1) * QB].transpose([1, 0, 2]),
                  rd=[f"d_xnT{nb}"], wr=["XSL"])
            if rope:
                P.dma("sync", rsl_c, CONST_IN["COS128"][:, nb * QB:(nb + 1) * QB], wr=["RSL"])
                P.dma("sync", rsl_s, CONST_IN["SINSIG"][:, nb * QB:(nb + 1) * QB], wr=["RSL"])
            for c in range(NCH):
                slot = c % 2
                rt0, rt1, kt0, kt1 = rpair[slot]
                psq, pk = (PSW["1L"], "pW1L") if slot == 0 else (PSW["1R"], "pW1R")
                stream_w_tiles(wk3d, c, slot)
                wt = WST[:, slot * 1024:(slot + 1) * 1024].rearrange("p (k f) -> p k f", f=128)
                for k in range(NCH):
                    P.step("tensor", lambda e, k=k, wt=wt, psq=psq: e.matmul(
                        psq, wt[:, k, :], xsl[:, k, :],
                        start=(k == 0), stop=(k == NCH - 1)),
                        rd=[f"WST{slot}", "XSL"], wr=[pk])
                P.step("vector", lambda e, c=c, psq=psq, rt0=rt0: e.tensor_scalar(
                    rt0, psq, SBUF_BIAS[:, bk_col + c:bk_col + c + 1], None, ALU.add),
                    rd=[pk, "BB"], wr=[kt0])
                if rope:
                    swap_dmas(rt0, rt1, kt0, kt1)
                    P.step("vector", lambda e, rt0=rt0: e.tensor_tensor(rt0, rt0, rsl_c, ALU.mult),
                           rd=[kt0, "RSL"], wr=[kt0])
                    P.step("vector", lambda e, rt1=rt1: e.tensor_tensor(rt1, rt1, rsl_s, ALU.mult),
                           rd=[kt1, "RSL"], wr=[kt1])
                    P.step("vector", lambda e, c=c, nb=nb, rt0=rt0, rt1=rt1: e.tensor_tensor(
                        Kb[:, c, nb * QB:(nb + 1) * QB], rt0, rt1, ALU.add),
                        rd=[kt0, kt1], wr=["K"])
                else:
                    P.step("vector", lambda e, c=c, nb=nb, rt0=rt0: e.tensor_copy(
                        Kb[:, c, nb * QB:(nb + 1) * QB], rt0), rd=[kt0], wr=["K"])
            for j in range(QB // 128):
                i = nb * (QB // 128) + j
                for half in range(2):
                    psq, pk = (PSW["0L"], "pW0L") if half == 0 else (PSW["0R"], "pW0R")
                    for k in range(NCH):
                        P.step("tensor", lambda e, j=j, k=k, psq=psq, half=half: e.matmul(
                            psq, xsl[:, k, j * 128:(j + 1) * 128],
                            AR(AR_WV, 8192)[:, k * 1024 + half * 512:k * 1024 + half * 512 + 512],
                            start=(k == 0), stop=False),
                            rd=["XSL", "WV"], wr=[pk])
                    P.step("tensor", lambda e, half=half, psq=psq: e.matmul(
                        psq, CONSTS["ONESROW"][0:1, :],
                        SROWS16[0:1, vb_row + half * 512:vb_row + (half + 1) * 512],
                        start=False, stop=True), rd=["ROW160", "CONST"], wr=[pk])
                    P.step("vector", lambda e, i=i, half=half, psq=psq: e.tensor_copy(
                        Vv[:, i, :].rearrange("p (h x) -> p h x", x=65)[:, half * 8:half * 8 + 8, 0:64],
                        psq.rearrange("p (h x) -> p h x", x=64)), rd=[pk], wr=["V"])

    def load_qblock_inputs(qb, rope):
        xsl = AR(AR_XSL, 4096).rearrange("p (c n) -> p c n", n=QB)
        P.dma("sync", xsl, xnT_hbm[:, :, qb * QB:(qb + 1) * QB].transpose([1, 0, 2]),
              rd=[f"d_xnT{qb}"], wr=["XSL"])
        if rope:
            P.dma("sync", AR(AR_RSL, 512), CONST_IN["COS128"][:, qb * QB:(qb + 1) * QB],
                  wr=["RSL"])
            P.dma("sync", AR(AR_RSL + 512, 512), CONST_IN["SINSIG"][:, qb * QB:(qb + 1) * QB],
                  wr=["RSL"])

    def attn_qblock(layer, qb, wq3d, bq_col, rope, wo_b_row, src_x, srckey, dst_x, dstkey, prefetch=None):
        xsl = AR(AR_XSL, 4096).rearrange("p (c n) -> p c n", n=QB)
        qsl = AR(AR_QSL, 4096).rearrange("p (c n) -> p c n", n=QB)
        rsl_c = AR(AR_RSL, 512)
        rsl_s = AR(AR_RSL + 512, 512)
        rt0, rt1 = AR(AR_RT0, 512), AR(AR_RT1, 512)
        for c in range(NCH):
            slot = c % 2
            psq, pk = (PSC, "pC") if slot == 0 else (PSD, "pD")
            stream_w_tiles(wq3d, c, slot)
            wt = WST[:, slot * 1024:(slot + 1) * 1024].rearrange("p (k f) -> p k f", f=128)
            for k in range(NCH):
                P.step("tensor", lambda e, k=k, wt=wt, psq=psq: e.matmul(
                    psq[:, :], wt[:, k, :], xsl[:, k, :],
                    start=(k == 0), stop=(k == NCH - 1)), rd=[f"WST{slot}", "XSL"], wr=[pk])
            P.step("vector", lambda e, c=c, psq=psq: e.tensor_scalar(
                rt0, psq[:, :], SBUF_BIAS[:, bq_col + c:bq_col + c + 1], None, ALU.add),
                rd=[pk, "BB"], wr=["RT0"])
            if rope:
                swap_dmas(rt0, rt1, "RT0", "RT1")
                P.step("vector", lambda e: e.tensor_tensor(rt0, rt0, rsl_c, ALU.mult),
                       rd=["RT0", "RSL"], wr=["RT0"])
                P.step("vector", lambda e: e.tensor_tensor(rt1, rt1, rsl_s, ALU.mult),
                       rd=["RT1", "RSL"], wr=["RT1"])
                P.step("vector", lambda e, c=c: e.tensor_tensor(qsl[:, c, :], rt0, rt1, ALU.add),
                       rd=["RT0", "RT1"], wr=["QSL"])
            else:
                P.step("vector", lambda e, c=c: e.tensor_copy(qsl[:, c, :], rt0),
                       rd=["RT0"], wr=["QSL"])
        Kb = AR(AR_K, 32768).rearrange("p (c n) -> p c n", n=N)
        Vv = AR(AR_V, 33280).bitcast(FP16).rearrange("p (t g) -> p t g", g=1040)
        PBs = [(AR(AR_PB, 1024).bitcast(FP16), ["PBa", "PBb"]),
               (AR(AR_RT0, 1024).bitcast(FP16), ["RT0", "RT1"]),
               (AR(AR_RSL, 1024).bitcast(FP16), ["RSL"])]
        pb_depth = 2 if rope else 3
        ctx = AR(AR_CTX, 4096)
        ctxT = AR(AR_CTXT, 512)
        kt_max = (qb + 1) * (QB // 128) if layer == 0 else NT

        def emit_S(p, kt):
            SW, swL, swR = (PSW0, "pW0L", "pW0R") if kt % 2 == 0 else (PSW1, "pW1L", "pW1R")
            P.step("tensor", lambda e, p=p, kt=kt, SW=SW: e.matmul(
                SW[:, 0:512], Kb[0:64, p, kt * 128:(kt + 1) * 128], qsl[0:64, p, :],
                start=True, stop=True, tile_position=(0, 0)), rd=["K", "QSL"], wr=[swL])
            P.step("tensor", lambda e, p=p, kt=kt, SW=SW: e.matmul(
                SW[:, 512:1024], Kb[64:128, p, kt * 128:(kt + 1) * 128], qsl[64:128, p, :],
                start=True, stop=True, tile_position=(64, 0)), rd=["K", "QSL"], wr=[swR])

        def copy_out(p):
            P.step("vector", lambda e: e.tensor_copy(STG[0:65, 0:512], PSA[0:65, :]),
                   rd=["pA"], wr=["STGa"])
            P.step("vector", lambda e: e.tensor_copy(STG[0:65, 512:1024], PSB[0:65, :]),
                   rd=["pB"], wr=["STGb"])

        def make_norm(p):
            def norm():
                P.step("vector", lambda e: e.reciprocal(REPB[64:65, 0:512], STG[64:65, 0:512]),
                       rd=["STGa"], wr=["REPB"])
                P.step("vector", lambda e: e.reciprocal(REPB[64:65, 512:1024], STG[64:65, 512:1024]),
                       rd=["STGb"], wr=["REPB"])
                P.step("tensor", lambda e: e.matmul(PSC[0:64, :], CONSTS["LTRI"][64:65, 64:128],
                                                    REPB[64:65, 0:512], start=True, stop=True),
                       rd=["REPB", "CONST"], wr=["pC"])
                P.step("tensor", lambda e: e.matmul(PSD[0:64, :], CONSTS["LTRI"][64:65, 64:128],
                                                    REPB[64:65, 512:1024], start=True, stop=True),
                       rd=["REPB", "CONST"], wr=["pD"])
                P.step("vector", lambda e: e.tensor_copy(STG[0:64, 1024:1536], PSC[0:64, :]),
                       rd=["pC"], wr=["STGc"])
                P.step("vector", lambda e: e.tensor_copy(STG[0:64, 1536:2048], PSD[0:64, :]),
                       rd=["pD"], wr=["STGd"])
                P.step("vector", lambda e, p=p: e.tensor_tensor(
                    ctx[0:64, p * 512:(p + 1) * 512], STG[0:64, 0:512], STG[0:64, 1024:1536],
                    ALU.mult), rd=["STGa", "STGc"], wr=["CTXe"])
                P.step("vector", lambda e: e.tensor_tensor(
                    ctxT[0:64, :], STG[0:64, 512:1024], STG[0:64, 1536:2048], ALU.mult),
                    rd=["STGb", "STGd"], wr=["CTT"])
                P.dma("sync", ctx[64:128, p * 512:(p + 1) * 512], ctxT[0:64, :],
                      rd=["CTT"], wr=["CTXe"])
            return norm

        pending_norm = None
        for p in range(NCH):
            emit_S(p, 0)
            for kt in range(kt_max):
                SW, swL, swR = (PSW0, "pW0L", "pW0R") if kt % 2 == 0 else (PSW1, "pW1L", "pW1R")
                PB, pbk = PBs[kt % pb_depth]
                if kt + 1 < kt_max:
                    emit_S(p, kt + 1)
                P.step("scalar", lambda e, SW=SW, PB=PB: e.activation(
                    PB[:, :], SW[:, :], AF.Exp, scale=0.125), rd=[swL, swR], wr=pbk)
                if layer == 0 and kt >= qb * 4:
                    jb = kt - qb * 4
                    for h in range(2):
                        P.step("gpsimd", lambda e, h=h, jb=jb, PB=PB: e.affine_select(
                            PB[:, h * 512:(h + 1) * 512], PB[:, h * 512:(h + 1) * 512],
                            [[1, 512]], ALU.is_ge, 0.0, base=-jb * 128, channel_multiplier=-1),
                            rd=pbk, wr=pbk)
                if layer == 1 and qb == 0 and kt == NT - 1:
                    for h in range(2):
                        P.step("gpsimd", lambda e, h=h, PB=PB: e.affine_select(
                            PB[:, h * 512:(h + 1) * 512], PB[:, h * 512:(h + 1) * 512],
                            [[1, 512]], ALU.is_gt, 0.0, base=127, channel_multiplier=-1),
                            rd=pbk, wr=pbk)
                if layer == 1 and qb == NQB - 1 and kt == 0:
                    for h in range(2):
                        P.step("gpsimd", lambda e, h=h, PB=PB: e.affine_select(
                            PB[:, h * 512:(h + 1) * 512], PB[:, h * 512:(h + 1) * 512],
                            [[-1, 512]], ALU.is_gt, 0.0, base=511, channel_multiplier=1),
                            rd=pbk, wr=pbk)
                P.step("tensor", lambda e, p=p, kt=kt, kt_max=kt_max, PB=PB: e.matmul(
                    PSA[0:65, :], Vv[:, kt, 2 * p * 65:2 * p * 65 + 65], PB[:, 0:512],
                    start=(kt == 0), stop=(kt == kt_max - 1)), rd=["V"] + pbk, wr=["pA"])
                P.step("tensor", lambda e, p=p, kt=kt, kt_max=kt_max, PB=PB: e.matmul(
                    PSB[0:65, :], Vv[:, kt, (2 * p + 1) * 65:(2 * p + 1) * 65 + 65],
                    PB[:, 512:1024],
                    start=(kt == 0), stop=(kt == kt_max - 1)), rd=["V"] + pbk, wr=["pB"])
                if kt == 2 and pending_norm is not None:
                    pending_norm()
                    pending_norm = None
            if pending_norm is not None:
                pending_norm()
                pending_norm = None
            copy_out(p)
            pending_norm = make_norm(p)
            if p == 0 and prefetch is not None:
                prefetch()
        pending_norm()
        Ct = ctx.rearrange("p (pp n) -> p pp n", n=512)
        WOr = AR(AR_WV, 8192)
        for j in range(QB // 128):
            ti = qb * (QB // 128) + j
            for half in range(2):
                psq, pk = (PSW["0L"], "pW0L") if half == 0 else (PSW["0R"], "pW0R")
                for p in range(NCH):
                    P.step("tensor", lambda e, j=j, p=p, half=half, psq=psq: e.matmul(
                        psq, Ct[:, p, j * 128:(j + 1) * 128],
                        WOr[:, p * 1024 + half * 512:p * 1024 + half * 512 + 512],
                        start=(p == 0), stop=(p == NCH - 1 and wo_b_row is None)),
                        rd=["CTXe", "WV"], wr=[pk])
                if wo_b_row is not None:
                    P.step("tensor", lambda e, half=half, psq=psq: e.matmul(
                        psq, CONSTS["ONESROW"][0:1, :],
                        SROWS16[0:1, wo_b_row + half * 512:wo_b_row + (half + 1) * 512],
                        start=False, stop=True), rd=["ROW160", "CONST"], wr=[pk])
            P.dma("sync", XST[:, :], src_x[ti * 128:(ti + 1) * 128, :],
                  rd=[srckey if srckey != "d_xb" else f"d_xb{ti // 4}"], wr=["XST"])
            P.step("vector", lambda e: e.tensor_tensor(STG[:, 0:512], XST[:, 0:512], PSW["0L"],
                                                       ALU.add), rd=["XST", "pW0L"], wr=["STGa"])
            P.step("vector", lambda e: e.tensor_tensor(STG[:, 512:1024], XST[:, 512:1024], PSW["0R"],
                                                       ALU.add), rd=["XST", "pW0R"], wr=["STGb"])
            P.dma("sync", dst_x[ti * 128:(ti + 1) * 128, :], STG[:, 0:1024],
                  rd=["STGa", "STGb"], wr=[dstkey])

    def ffn_l0():
        P.dma("gpsimd", AR(AR_W1H, 32768).rearrange("p (c f) -> p c f", f=DH),
              W["L0_ffn_heavy_w1"].rearrange("(c p) d -> c p d", p=128).transpose([1, 0, 2]),
              wr=["W1H", "K", "T0", "T1"])
        for c in range(NCH):
            P.step("vector", lambda e, c=c: e.tensor_scalar(
                AR(AR_W1H + c * DH, DH), AR(AR_W1H + c * DH, DH),
                SBUF_BIAS[:, BC["hgam"] + c:BC["hgam"] + c + 1], None, ALU.mult),
                rd=["W1H", "BB"], wr=["W1H"])
        P.dma("gpsimd", AR(AR_W2H, 32768).rearrange("p (c f) -> p c f", f=1024),
              W["L0_ffn_heavy_w2"].rearrange("(c p) d -> c p d", p=128).transpose([1, 0, 2]),
              wr=["W2H", "V", "WV"])
        P.dma("gpsimd", AR(AR_LW1, 4096).rearrange("p (c f) -> p c f", f=DL),
              W["L0_ffn_light_w1"].rearrange("(c p) d -> c p d", p=128).transpose([1, 0, 2]),
              rd=["QSL", "XSL"], wr=["LW1"])
        for c in range(NCH):
            P.step("vector", lambda e, c=c: e.tensor_scalar(
                AR(AR_LW1 + c * DL, DL), AR(AR_LW1 + c * DL, DL),
                SBUF_BIAS[:, BC["lgam"] + c:BC["lgam"] + c + 1], None, ALU.mult),
                rd=["LW1", "BB"], wr=["LW1"])
        P.dma("gpsimd", AR(AR_LW2, 4096).rearrange("p (c f) -> p c f", f=1024),
              W["L0_ffn_light_w2"].rearrange("(c p) d -> c p d", p=128).transpose([1, 0, 2]),
              rd=["XSL"], wr=["LW2", "CTXe", "CTXa0", "CTXa1", "CTXb0", "CTXb1"])
        w1t = AR(AR_W1H, 32768).rearrange("p (c f) -> p c f", f=DH)
        w2c = AR(AR_W2H, 32768).rearrange("p (c f) -> p c f", f=1024)
        lw1 = AR(AR_LW1, 4096).rearrange("p (c f) -> p c f", f=DL)
        lw2 = AR(AR_LW2, 4096).rearrange("p (c f) -> p c f", f=1024)
        Hc = AR(AR_HB, 8192).rearrange("p (c n) -> p c n", n=FB)
        LHc = AR(AR_LHB, 1024).rearrange("p (c n) -> p c n", n=FB)
        xr = AR(AR_XRB, 2048).rearrange("p (c n) -> p c n", n=FB)
        hps = [(PSW["0L"], "pW0L"), (PSW["0R"], "pW0R"), (PSW["1L"], "pW1L"), (PSW["1R"], "pW1R")]
        for fb in range(NFB):
            P.dma("sync", xr, xrT_hbm[:, :, fb * FB:(fb + 1) * FB].transpose([1, 0, 2]),
                  rd=[f"d_xrT{fb}", "QSL"], wr=["XRB"])
            for dh in range(DH // 128):
                psq, pk = hps[dh % 4]
                hk = f"HB{dh % 2}"
                for k in range(NCH):
                    P.step("tensor", lambda e, dh=dh, k=k, psq=psq: e.matmul(
                        psq[:, 0:FB], w1t[:, k, dh * 128:(dh + 1) * 128], xr[:, k, :],
                        start=(k == 0), stop=(k == NCH - 1)), rd=["W1H", "XRB"], wr=[pk])
                P.step("scalar", lambda e, dh=dh, psq=psq: e.activation(
                    Hc[:, dh, :], psq[:, 0:FB], AF.Gelu,
                    bias=SBUF_BIAS[:, BC["hb1"] + dh:BC["hb1"] + dh + 1]),
                    rd=[pk, "BB", "V"], wr=[hk])
            for dl in range(DL // 128):
                psq, pk = hps[dl % 4]
                for k in range(NCH):
                    P.step("tensor", lambda e, dl=dl, k=k, psq=psq: e.matmul(
                        psq[:, 0:FB], lw1[:, k, dl * 128:(dl + 1) * 128], xr[:, k, :],
                        start=(k == 0), stop=(k == NCH - 1)), rd=["LW1", "XRB"], wr=[pk])
                P.step("scalar", lambda e, dl=dl, psq=psq: e.activation(
                    LHc[:, dl, :], psq[:, 0:FB], AF.Gelu,
                    bias=SBUF_BIAS[:, BC["lb1"] + dl:BC["lb1"] + dl + 1]),
                    rd=[pk, "BB", "V", "QSL"], wr=["LHB"])
            for j in range(FB // 128):
                ti = fb * (FB // 128) + j
                for half in range(2):
                    psq, pk = (PSC, "pC") if half == 0 else (PSD, "pD")
                    psh, phk = (PSA, "pA") if half == 0 else (PSB, "pB")
                    for dl in range(DL // 128):
                        P.step("tensor", lambda e, dl=dl, j=j, half=half, psq=psq: e.matmul(
                            psq[:, :], LHc[:, dl, j * 128:(j + 1) * 128],
                            lw2[:, dl, half * 512:(half + 1) * 512],
                            start=(dl == 0), stop=False), rd=["LHB", "LW2"], wr=[pk])
                    P.step("tensor", lambda e, half=half, psq=psq: e.matmul(
                        psq[:, :], CONSTS["ONESROW"][0:1, :],
                        SROWS16[0:1, ROW_OFF["lb2"] + half * 512:ROW_OFF["lb2"] + (half + 1) * 512],
                        start=False, stop=True), rd=["ROW160", "CONST"], wr=[pk])
                    for dh in range(DH // 128):
                        P.step("tensor", lambda e, dh=dh, j=j, half=half, psh=psh: e.matmul(
                            psh[:, :], Hc[:, dh, j * 128:(j + 1) * 128],
                            w2c[:, dh, half * 512:(half + 1) * 512],
                            start=(dh == 0), stop=False), rd=[f"HB{dh % 2}", "W2H"], wr=[phk])
                    P.step("tensor", lambda e, half=half, psh=psh: e.matmul(
                        psh[:, :], CONSTS["ONESROW"][0:1, :],
                        SROWS16[0:1, ROW_OFF["hb2"] + half * 512:ROW_OFF["hb2"] + (half + 1) * 512],
                        start=False, stop=True), rd=["ROW161", "CONST"], wr=[phk])
                P.dma("sync", XST[:, :], xprime[ti * 128:(ti + 1) * 128, :],
                      rd=["d_xp"], wr=["XST"])
                P.step("vector", lambda e: e.tensor_tensor(STG[:, 0:512], XST[:, 0:512], PSC[:, :],
                                                           ALU.add), rd=["XST", "pC"], wr=["STGa"])
                P.step("vector", lambda e: e.tensor_tensor(STG[:, 512:1024], XST[:, 512:1024],
                                                           PSD[:, :], ALU.add),
                       rd=["XST", "pD"], wr=["STGb"])
                P.step("vector", lambda e, ti=ti: e.scalar_tensor_tensor(
                    STG[:, 0:512], PSA[:, :], RBUF[:, 32 + ti:33 + ti], STG[:, 0:512],
                    ALU.mult, ALU.add), rd=["pA", "RB", "STGa"], wr=["STGa"])
                P.step("vector", lambda e, ti=ti: e.scalar_tensor_tensor(
                    STG[:, 512:1024], PSB[:, :], RBUF[:, 32 + ti:33 + ti], STG[:, 512:1024],
                    ALU.mult, ALU.add), rd=["pB", "RB", "STGb"], wr=["STGb"])
                P.dma("sync", x_buf[ti * 128:(ti + 1) * 128, :], STG[:, 0:1024],
                      rd=["STGa", "STGb"], wr=[f"d_xb{ti // 4}"])

    def transpose_l1_weights():
        t0 = AR(0, 1024)
        t1 = AR(1024, 1024)
        for nm, src in [("L1q", W["L1_attn_Wq"]), ("L1k", W["L1_attn_Wk"]),
                        ("L1v", W["L1_attn_Wv"]), ("L1o", W["L1_attn_Wo"])]:
            for fc in range(NCH):
                P.dma("gpsimd", t0.rearrange("p (c f) -> p c f", f=128),
                      src.rearrange("(c p) (k f) -> c p k f", p=128, f=128)[fc],
                      wr=["T0", "K"])
                for k in range(NCH):
                    psq, pk = (PSC, "pC") if k % 2 == 0 else (PSD, "pD")
                    P.step("tensor", lambda e, k=k, psq=psq: e.matmul(
                        psq[:, 0:128], t0[:, k * 128:(k + 1) * 128],
                        CONSTS["IDENT"][:, :], start=True, stop=True),
                        rd=["T0", "CONST"], wr=[pk])
                    P.step("vector", lambda e, k=k, psq=psq: e.tensor_copy(
                        t1[:, k * 128:(k + 1) * 128], psq[:, 0:128]), rd=[pk], wr=["T1", "K"])
                P.dma("sync", wT_hbm[nm][:, :, fc * 128:(fc + 1) * 128].transpose([1, 0, 2]),
                      t1.rearrange("p (k f) -> p k f", f=128), rd=["T1"], wr=["d_wT"])

    def ffn_l1():
        P.barrier()
        P.dma("gpsimd", AR(AR_WIN, 512).rearrange("p (c f) -> p c f", f=64),
              W["L1_ffn_Win"].rearrange("(c p) m -> c p m", p=128).transpose([1, 0, 2]),
              wr=["WIN"])
        P.dma("gpsimd", AR(AR_WOUT, 1024)[0:64, :], W["L1_ffn_Wout"][:, :], wr=["WOU"])
        win = AR(AR_WIN, 512).rearrange("p (c f) -> p c f", f=64)
        wout = AR(AR_WOUT, 1024)
        xsl = AR(AR_XSL, 4096).rearrange("p (c n) -> p c n", n=QB)
        sbf = AR(AR_SBF, 512)
        hf = AR(AR_HF, 4096)
        gbf = AR(AR_GBF, 512)
        for qb in range(NQB):
            P.dma("sync", xsl, xnT_hbm[:, :, qb * QB:(qb + 1) * QB].transpose([1, 0, 2]),
                  rd=[f"d_xnT{qb}"], wr=["XSL"])
            for k in range(NCH):
                P.step("tensor", lambda e, k=k: e.matmul(
                    PSC[0:64, :], win[:, k, :], xsl[:, k, :],
                    start=(k == 0), stop=(k == NCH - 1)), rd=["WIN", "XSL"], wr=["pC"])
            P.step("vector", lambda e: e.tensor_scalar(sbf[0:64, :], PSC[0:64, :],
                                                       SBUF_BIAS[0:64, BC["bin"]:BC["bin"] + 1],
                                                       None, ALU.add),
                   rd=["pC", "BB"], wr=["SBF"])
            for c in range(NCH):
                P.step("tensor", lambda e, c=c: e.matmul(
                    PSD[:, :], CONSTS["R0"][0:64, c * 128:(c + 1) * 128], sbf[0:64, :],
                    start=True, stop=True), rd=["CONST", "SBF"], wr=["pD"])
                P.step("scalar", lambda e, c=c: e.activation(
                    hf[:, c * 512:(c + 1) * 512], PSD[:, :], AF.Relu,
                    bias=SBUF_BIAS[:, BC["b1f"] + c:BC["b1f"] + c + 1],
                    scale=SBUF_BIAS[:, BC["w1f"] + c:BC["w1f"] + c + 1]),
                    rd=["pD", "BB"], wr=["HF"])
                P.step("vector", lambda e, c=c: e.tensor_scalar(
                    hf[:, c * 512:(c + 1) * 512], hf[:, c * 512:(c + 1) * 512],
                    SBUF_BIAS[:, BC["w2f"] + c:BC["w2f"] + c + 1], None, ALU.mult),
                    rd=["HF", "BB"], wr=["HF"])
            for c in range(NCH):
                P.step("tensor", lambda e, c=c: e.matmul(
                    PSA[0:64, :], CONSTS["C0"][:, c * 64:(c + 1) * 64],
                    hf[:, c * 512:(c + 1) * 512],
                    start=(c == 0), stop=(c == NCH - 1)), rd=["CONST", "HF"], wr=["pA"])
            P.step("vector", lambda e: e.tensor_scalar(gbf[0:64, :], PSA[0:64, :],
                                                       SBUF_BIAS[0:64, BC["b2"]:BC["b2"] + 1],
                                                       None, ALU.add),
                   rd=["pA", "BB"], wr=["GBF"])
            for j in range(QB // 128):
                ti = qb * (QB // 128) + j
                for half in range(2):
                    psq, pk = (PSW["0L"], "pW0L") if half == 0 else (PSW["0R"], "pW0R")
                    P.step("tensor", lambda e, j=j, half=half, psq=psq: e.matmul(
                        psq, gbf[0:64, j * 128:(j + 1) * 128],
                        wout[0:64, half * 512:(half + 1) * 512], start=True, stop=False),
                        rd=["GBF", "WOU"], wr=[pk])
                    P.step("tensor", lambda e, half=half, psq=psq: e.matmul(
                        psq, CONSTS["ONESROW"][0:1, :],
                        SROWS16[0:1, ROW_OFF["bout"] + half * 512:ROW_OFF["bout"] + (half + 1) * 512],
                        start=False, stop=True), rd=["ROW160", "CONST"], wr=[pk])
                P.dma("sync", XST[:, :], xprime[ti * 128:(ti + 1) * 128, :],
                      rd=["d_xp"], wr=["XST"])
                P.step("vector", lambda e: e.tensor_tensor(STG[:, 0:512], XST[:, 0:512],
                                                           PSW["0L"], ALU.add),
                       rd=["XST", "pW0L"], wr=["STGa"])
                P.step("vector", lambda e: e.tensor_tensor(STG[:, 512:1024], XST[:, 512:1024],
                                                           PSW["0R"], ALU.add),
                       rd=["XST", "pW0R"], wr=["STGb"])
                P.dma("sync", y_out[ti * 128:(ti + 1) * 128, :], STG[:, 0:1024],
                      rd=["STGa", "STGb"], wr=["d_y"])

    def copy_dram(dst, src, skey, dkey):
        P.barrier()
        for i in range(NT):
            P.dma("sync", XST[:, :], src[i * 128:(i + 1) * 128, :], rd=[skey], wr=["XST"])
            P.step("vector", lambda e: e.tensor_copy(STG[:, 0:1024], XST[:, :]),
                   rd=["XST"], wr=["STGa", "STGb"])
            P.dma("sync", dst[i * 128:(i + 1) * 128, :], STG[:, 0:1024],
                  rd=["STGa", "STGb"], wr=[dkey])

    def finish():
        with nc.Block() as blk:
            P.emit(nc, blk, sems)
        return nc

    for k in CONSTS:
        P.dma("sync", CONSTS[k][0:CONSTS[k].shape[0], :], CONST_IN[k][:, :], wr=["CONST"])

    bc_alloc("wqb", 8); bc_alloc("wkb", 8)
    bc_alloc("hb1", 32); bc_alloc("lb1", 4)
    bc_alloc("hgam", 8); bc_alloc("lgam", 8)
    bc_alloc("bq1", 8); bc_alloc("bk1", 8)
    bc_alloc("bin", 1); bc_alloc("b2", 1)
    bc_alloc("w1f", 8); bc_alloc("b1f", 8); bc_alloc("w2f", 8)
    load_vec_chunks(W["L0_attn_wq_b"], D, BC["wqb"])
    load_vec_chunks(W["L0_attn_wk_b"], D, BC["wkb"])
    load_vec_chunks(W["L0_ffn_heavy_b1"], DH, BC["hb1"])
    load_vec_chunks(W["L0_ffn_light_b1"], DL, BC["lb1"])
    load_vec_chunks(W["L0_ffn_heavy_gamma"], D, BC["hgam"])
    load_vec_chunks(W["L0_ffn_light_gamma"], D, BC["lgam"])
    load_vec_chunks(W["L1_attn_bq"], D, BC["bq1"])
    load_vec_chunks(W["L1_attn_bk"], D, BC["bk1"])
    load_vec_chunks(W["L1_ffn_W1"].rearrange("a b -> (a b)"), D, BC["w1f"])
    load_vec_chunks(W["L1_ffn_b1"].rearrange("a b -> (a b)"), D, BC["b1f"])
    load_vec_chunks(W["L1_ffn_W2"].rearrange("a b -> (a b)"), D, BC["w2f"])
    load_vec64_col(W["L1_ffn_bin_"], BC["bin"])
    load_vec64_col(W["L1_ffn_b2"], BC["b2"])

    phase_A(0, x_in, "d_x")
    if upto == "routing":
        phase_routing()
        P.step("vector", lambda e: e.tensor_copy(STG[:, 0:64], RBUF[:, 0:64]),
               rd=["RB"], wr=["STGa"])
        P.dma("sync", y_out[0:128, 0:64], STG[:, 0:64], rd=["STGa"], wr=["d_y"])
        P.dma("sync", y_out[128:256, 0:64], SMALL[:, 0:64], rd=["SM"], wr=["d_y"])
        P.dma("sync", y_out[256:384, 0:256], STG[:, 0:256], rd=["STGs", "STGt"], wr=["d_y"])
        return finish()
    load_row16(W["L0_attn_wv_b"], 0, "vb0")
    P.dma("gpsimd", AR(AR_WV, 8192).rearrange("p (c f) -> p c f", f=1024),
          W["L0_attn_wv_w"].rearrange("(c p) d -> c p d", p=128).transpose([1, 0, 2]),
          wr=["WV"])
    gen_KV(0, W["L0_attn_wk_w"].rearrange("(c p) d -> c p d", p=128), BC["wkb"], rope=True,
           vb_row=ROW_OFF["vb0"])
    phase_routing()
    load_row16(W["L0_attn_wo_b"], 0, "wob")
    P.dma("gpsimd", AR(AR_WV, 8192).rearrange("p (c f) -> p c f", f=1024),
          W["L0_attn_wo_w"].rearrange("(c p) d -> c p d", p=128).transpose([1, 0, 2]),
          rd=["WV"], wr=["WV"])
    load_qblock_inputs(0, True)
    for qb in range(NQB):
        pf = (lambda q=qb + 1: load_qblock_inputs(q, True)) if qb + 1 < NQB else None
        attn_qblock(0, qb, W["L0_attn_wq_w"].rearrange("(c p) d -> c p d", p=128), BC["wqb"],
                    True, ROW_OFF["wob"], x_in, "d_x", xprime, "d_xp", prefetch=pf)
    if upto == "attn0":
        copy_dram(y_out, xprime, "d_xp", "d_y")
        return finish()
    load_row16(W["L0_ffn_light_b2"], 0, "lb2")
    load_row16(W["L0_ffn_heavy_b2"], 1, "hb2")
    transpose_l1_weights()
    ffn_l0()
    if upto == "ffn0":
        copy_dram(y_out, x_buf, "d_xb7", "d_y")
        return finish()

    phase_A(1, x_buf, "d_xb")
    load_row16(W["L1_attn_bv"], 0, "vb1")
    P.barrier()
    P.dma("gpsimd", AR(AR_WV, 8192).rearrange("p (c f) -> p c f", f=1024),
          wT_hbm["L1v"].transpose([1, 0, 2]), rd=["d_wT"], wr=["WV"])
    gen_KV(1, wT_hbm["L1k"], BC["bk1"], rope=False, vb_row=ROW_OFF["vb1"])
    P.dma("gpsimd", AR(AR_WV, 8192).rearrange("p (c f) -> p c f", f=1024),
          wT_hbm["L1o"].transpose([1, 0, 2]), rd=["WV", "d_wT"], wr=["WV"])
    load_qblock_inputs(0, False)
    for qb in range(NQB):
        pf = (lambda q=qb + 1: load_qblock_inputs(q, False)) if qb + 1 < NQB else None
        attn_qblock(1, qb, wT_hbm["L1q"], BC["bq1"], False, None, x_buf, "d_xb",
                    xprime, "d_xp", prefetch=pf)
    if upto == "attn1":
        copy_dram(y_out, xprime, "d_xp", "d_y")
        return finish()
    load_row16(W["L1_ffn_bout"], 0, "bout")
    ffn_l1()
    return finish()


_prog_cache = {}


def _get_program():
    if "nc" not in _prog_cache:
        _prog_cache["nc"] = build_program()
    return _prog_cache["nc"]


def kernel(**inputs):
    nc = _get_program()
    consts = host_constants()
    x = np.ascontiguousarray(np.asarray(inputs["L0_attn_x"]), dtype=np.float32)
    base = {}
    for k, v in inputs.items():
        if k == "L0_attn_x":
            continue
        base[k] = np.ascontiguousarray(np.asarray(v), dtype=np.float32)
    for k, v in consts.items():
        base["c_" + k] = np.ascontiguousarray(v)
    in_maps = [dict(base, x=np.ascontiguousarray(x[i])) for i in range(8)]
    res = run_bass_kernel_spmd(nc, in_maps, core_ids=list(range(8)))
    return np.stack([np.asarray(r["y"]) for r in res.results]).astype(np.float32)
